# version 18
# baseline (speedup 1.0000x reference)
import numpy as np
import ml_dtypes
from contextlib import ExitStack

import concourse.bass as bass
import concourse.mybir as mybir
from concourse.bass_utils import run_bass_kernel_spmd
from concourse.alu_op_type import AluOpType

F32 = mybir.dt.float32
BF16 = mybir.dt.bfloat16
AF = mybir.ActivationFunctionType
AX = mybir.AxisListType
BF = ml_dtypes.bfloat16

D = 2048
S = 2048
H = 16
DK = 128
NT = 16
NCH = 4
W = 512
INV_SQRT_DK = 1.0 / (DK ** 0.5)
NEG_BIG = -1.0e30
EPS = 1e-5

SEMS = {"t": 1, "v": 1, "a": 1, "p": 1,
        "cst": 16, "xbf": 16, "kvw": 16,
        "qw0": 16, "qw1": 16, "wo0": 16, "wo1": 16,
        "fw0": 16, "fw1": 16, "xr0": 16, "xr1": 16, "xr2": 16,
        "wob": 16, "fwb": 16, "od": 16}


ENG_SEM = {"tensor": "t", "vector": "v", "scalar": "a", "gpsimd": "p",
           "sync": None}


class Prog:

    def __init__(self, serial=False):
        self.ops = []
        self.counts = {k: 0 for k in SEMS}
        self.serial = serial
        self.last = None

    def op(self, eng, fn, waits=(), inc=None):
        w = [e for e in waits if e is not None]
        if self.serial:
            if self.last is not None:
                w = w + [self.last]
            if inc is None and ENG_SEM[eng] is not None:
                inc = ENG_SEM[eng]
        ev = None
        if inc is not None:
            self.counts[inc] += SEMS[inc]
            ev = (inc, self.counts[inc])
        self.ops.append((eng, fn, w, inc))
        if self.serial and ev is not None:
            self.last = ev
        return ev

    def emit(self, eng_name, eng, sems):
        hi = {}
        for idx, (name, fn, waits, inc) in enumerate(self.ops):
            if name != eng_name:
                continue
            for s, v in waits:
                if hi.get(s, 0) < v:
                    eng.wait_ge(sems[s], v)
                    hi[s] = v
            ins = fn(eng)
            if inc is not None:
                if ins is None:
                    ins = eng.nop()
                ins.then_inc(sems[inc], SEMS[inc])


def build_program(serial=False):
    nc = bass.Bass("TRN2", target_bir_lowering=False)

    def dram(name, shape, dt, kind="ExternalInput"):
        return nc.dram_tensor(name, shape, dt, kind=kind).ap()

    xT_h = dram("xT", [D, S], F32)
    qw_h = dram("qw", [D, D], F32)
    kw_h = dram("kw", [128, D], F32)
    vw_h = dram("vw", [128, D], F32)
    wo_h = dram("wo", [D, D], F32)
    fw_h = dram("fw", [D, D], F32)
    qb_h = dram("qb", [128, NT], F32)
    ob_h = dram("ob", [128, NT], F32)
    fb_h = dram("fb", [128, NT], F32)
    fsc_h = dram("fsc", [128, NT], F32)
    kb_h = dram("kb", [128, 1], F32)
    vb_h = dram("vb", [128, 1], F32)
    tri_h = dram("tri", [128, 128], BF16)
    ident_h = dram("ident", [128, 128], BF16)
    onesc_h = dram("onesc", [128, 1], BF16)
    onesrF_h = dram("onesrF", [33, 128], F32)
    out_h = dram("outT", [D, S], F32, kind="ExternalOutput")
    wo_bf = dram("wo_bf", [D, D], BF16, kind="Internal")
    fw_bf = dram("fw_bf", [D, D], BF16, kind="Internal")

    p = Prog(serial=serial)
    st = ExitStack()
    sb = lambda n, s, d: st.enter_context(nc.sbuf_tensor(n, s, d))
    ps = lambda n, s, d: st.enter_context(nc.psum_tensor(n, s, d))

    with st:
        onesc_s = sb("onesc_s", [128, 1], BF16)
        onesrF_s = sb("onesrF_s", [33, 128], F32)
        qb_s = sb("qb_s", [128, NT], F32)
        qbs_s = sb("qbs_s", [128, NT], F32)
        ob_s = sb("ob_s", [128, NT], F32)
        fb_s = sb("fb_s", [128, NT], F32)
        fsc_s = sb("fsc_s", [128, NT], F32)
        kb_s = sb("kb_s", [128, 1], F32)
        vb_s = sb("vb_s", [128, 1], F32)
        rowb = sb("rowb", [33, 1024], F32)
        eps1 = sb("eps1", [33, 1], F32)
        abc1_s = sb("abc1_s", [128, W], F32)
        bbc1_s = sb("bbc1_s", [128, W], F32)
        abc2_s = sb("abc2_s", [128, W], F32)
        bbc2_s = sb("bbc2_s", [128, W], F32)
        wost = sb("wost", [128, 2 * D], BF16)
        xres = sb("xres", [128, 3 * W], F32)

        psS = ps("psS", [128, 2048], F32)
        psT0 = ps("psT0", [128, W], F32)
        psT1 = ps("psT1", [128, W], F32)
        psO = ps("psO", [128, W], F32)
        psU = ps("psU", [128, W], F32)
        psT = [psT0, psT1]

        sems = {k: st.enter_context(nc.semaphore(k)) for k in SEMS}

        pool_banks = [("psT0", psT0), ("psT1", psT1)]
        bank_evac = {"psT0": None, "psT1": None, "psO": None, "psU": None}
        rot_idx = [0]

        def use_bank(mm_emit, evac_emit):
            b = rot_idx[0] % len(pool_banks)
            rot_idx[0] += 1
            key, bk = pool_banks[b]
            mmw = [bank_evac[key]] if bank_evac[key] else []
            e_mm = mm_emit(bk, mmw)
            e_ev = evac_emit(bk, [e_mm])
            bank_evac[key] = e_ev
            return e_mm, e_ev

        with nc.sbuf_tensor("QTO", [128, NT * S], BF16) as QTO:
            stAB = ExitStack()
            ident_s = stAB.enter_context(
                nc.sbuf_tensor("ident_s", [128, 128], BF16))
            KT = stAB.enter_context(nc.sbuf_tensor("KT", [128, S], BF16))
            Vsb = stAB.enter_context(nc.sbuf_tensor("Vsb", [128, NT * 128], BF16))

            for dst, src in [(ident_s, ident_h),
                             (onesc_s, onesc_h), (onesrF_s, onesrF_h),
                             (qb_s, qb_h), (ob_s, ob_h), (fb_s, fb_h),
                             (fsc_s, fsc_h), (kb_s, kb_h), (vb_s, vb_h)]:
                E_consts = p.op("sync", lambda e, d=dst, s=src: e.dma_start(
                    out=d[:], in_=s), inc="cst")
            p.op("vector", lambda e: e.memset(eps1[:], EPS), inc="v")
            E_qbs = p.op("scalar", lambda e: e.mul(
                qbs_s[:], qb_s[:], INV_SQRT_DK), waits=[E_consts], inc="a")

            stA = ExitStack()
            with stA:
                xTbf = stA.enter_context(nc.sbuf_tensor("xTbf", [128, NT * S], BF16))
                qwst = stA.enter_context(nc.sbuf_tensor("qwst", [128, 2 * D], BF16))
                kwst = stA.enter_context(nc.sbuf_tensor("kwst", [128, NT * 128], BF16))
                vwst = stA.enter_context(nc.sbuf_tensor("vwst", [128, NT * 128], BF16))

                E_xbf = p.op("gpsimd", lambda e: e.dma_start(
                    out=xTbf[:].rearrange("p (t s) -> p t s", t=NT),
                    in_=xT_h.rearrange("(t p) s -> p t s", p=128)), inc="xbf")
                p.op("gpsimd", lambda e: e.dma_start(
                    out=kwst[:], in_=kw_h), inc="kvw")
                E_kvw = p.op("gpsimd", lambda e: e.dma_start(
                    out=vwst[:], in_=vw_h), inc="kvw")

                E_kevac = [None] * NCH
                for c in range(NCH):
                    def mmK(bk, w, c=c):
                        w = w + [E_xbf, E_kvw]
                        for dt in range(NT):
                            ev = p.op("tensor", lambda e, bk=bk, dt=dt, c=c: e.matmul(
                                bk[:],
                                kwst[:, dt * 128:(dt + 1) * 128],
                                xTbf[:, dt * S + c * W: dt * S + c * W + W],
                                start=(dt == 0), stop=(dt == NT - 1)),
                                waits=(w if dt == 0 else ()),
                                inc=("t" if dt == NT - 1 else None))
                        return ev

                    def evK(bk, w, c=c):
                        return p.op("scalar", lambda e, bk=bk, c=c: e.activation(
                            KT[:, c * W:(c + 1) * W], bk[:], AF.Identity,
                            bias=kb_s[:], scale=1.0),
                            waits=w + [E_consts], inc="a")
                    _, E_kevac[c] = use_bank(mmK, evK)

                E_vevac = [None] * NCH
                for c in range(NCH):
                    def mmV(bk, w, c=c):
                        w = w + [E_kvw]
                        for dt in range(NT):
                            ev = p.op("tensor", lambda e, bk=bk, dt=dt, c=c: e.matmul(
                                bk[:],
                                vwst[:, dt * 128:(dt + 1) * 128],
                                xTbf[:, dt * S + c * W: dt * S + c * W + W],
                                start=(dt == 0), stop=(dt == NT - 1)),
                                waits=(w if dt == 0 else ()),
                                inc=("t" if dt == NT - 1 else None))
                        return ev

                    def evV(bk, w, c=c):
                        return p.op("scalar", lambda e, bk=bk, c=c: e.activation(
                            QTO[:, 15 * S + c * W: 15 * S + (c + 1) * W],
                            bk[:], AF.Identity,
                            bias=vb_s[:], scale=1.0), waits=w, inc="a")
                    _, E_vevac[c] = use_bank(mmV, evV)

                E_vsb = None
                for g in range(4):
                    def mmVT(bk, w, g=g):
                        w = w + [E_vevac[g], E_consts]
                        for j in range(4):
                            kt = 4 * g + j
                            ev = p.op("tensor", lambda e, bk=bk, j=j, kt=kt: e.matmul(
                                bk[:, j * 128:(j + 1) * 128],
                                QTO[:, 15 * S + kt * 128:
                                    15 * S + (kt + 1) * 128],
                                ident_s[:], start=True, stop=True),
                                waits=(w if j == 0 else ()),
                                inc=("t" if j == 3 else None))
                        return ev

                    def evVT(bk, w, g=g):
                        return p.op("scalar", lambda e, bk=bk, g=g: e.copy(
                            Vsb[:, g * W:(g + 1) * W], bk[:]), waits=w, inc="a")
                    _, E_vsb = use_bank(mmVT, evVT)

                E_qw = [None] * NT
                E_qmm_last = [None] * NT
                E_qevac = [[None] * NCH for _ in range(NT)]
                for ot in range(NT):
                    wq = [E_qmm_last[ot - 2]] if ot >= 2 else []
                    E_qw[ot] = p.op("gpsimd", lambda e, ot=ot: e.dma_start(
                        out=qwst[:, (ot % 2) * D:(ot % 2) * D + D],
                        in_=qw_h[ot * 128:(ot + 1) * 128, :]),
                        waits=wq, inc="qw%d" % (ot % 2))
                    for c in range(NCH):
                        def mmQ(bk, w, ot=ot, c=c):
                            w = w + [E_qw[ot]]
                            for dt in range(NT):
                                ev = p.op("tensor", lambda e, bk=bk, dt=dt, ot=ot, c=c:
                                          e.matmul(
                                    bk[:],
                                    qwst[:, (ot % 2) * D + dt * 128:
                                         (ot % 2) * D + dt * 128 + 128],
                                    xTbf[:, dt * S + c * W: dt * S + c * W + W],
                                    start=(dt == 0), stop=(dt == NT - 1)),
                                    waits=(w if dt == 0 else ()),
                                    inc=("t" if dt == NT - 1 else None))
                            return ev

                        def evQ(bk, w, ot=ot, c=c):
                            if ot == 15:
                                w = w + [E_vsb]
                            return p.op("scalar", lambda e, bk=bk, ot=ot, c=c:
                                        e.activation(
                                QTO[:, ot * S + c * W: ot * S + c * W + W],
                                bk[:], AF.Identity,
                                bias=qbs_s[:, ot:ot + 1], scale=INV_SQRT_DK),
                                waits=w + [E_qbs], inc="a")
                        e_mm, e_ev = use_bank(mmQ, evQ)
                        E_qevac[ot][c] = e_ev
                        if c == NCH - 1:
                            E_qmm_last[ot] = e_mm
                E_Q_all = E_qmm_last[NT - 1]

            for q in range(4):
                p.op("gpsimd", lambda e, q=q: e.dma_start(
                    out=wo_bf[q * 512:(q + 1) * 512, :],
                    in_=wo_h[q * 512:(q + 1) * 512, :]), inc="wob")
                p.op("gpsimd", lambda e, q=q: e.dma_start(
                    out=fw_bf[q * 512:(q + 1) * 512, :],
                    in_=fw_h[q * 512:(q + 1) * 512, :]), inc="fwb")
            E_wob = ("wob", 64)
            E_fwb = ("fwb", 64)

            stB2 = ExitStack()
            trib_s = stB2.enter_context(
                nc.sbuf_tensor("trib_s", [128, 128], BF16))
            expS8 = stB2.enter_context(
                nc.sbuf_tensor("expS8", [128, 8 * S], BF16))
            expST2 = stB2.enter_context(
                nc.sbuf_tensor("expST2", [128, 2 * NT * W], BF16))
            diagP = stB2.enter_context(
                nc.sbuf_tensor("diagP", [128, 8 * 128], BF16))
            negm8 = stB2.enter_context(nc.sbuf_tensor("negm8", [128, 8], F32))
            den8 = stB2.enter_context(nc.sbuf_tensor("den8", [128, 8], F32))
            rcp8 = stB2.enter_context(nc.sbuf_tensor("rcp8", [128, 8], F32))
            mpart = stB2.enter_context(nc.sbuf_tensor("mpart", [128, 32], F32))
            dpart = stB2.enter_context(nc.sbuf_tensor("dpart", [128, 32], F32))

            E_trib = p.op("sync", lambda e: e.dma_start(
                out=trib_s[:], in_=tri_h), waits=[E_Q_all], inc="cst")

            ring_ap = [psS[:, s * W:(s + 1) * W] for s in range(4)] + [psU]
            ring_ev = [None] * 5
            ring_i = [0]

            E_comb2 = [[None] * 4, [None] * 4]
            E_dsum2 = [[None] * 4, [None] * 4]
            E_expl2 = [[None] * 4, [None] * 4]
            E_exall2 = [[[] for _ in range(4)], [[] for _ in range(4)]]
            E_diag2v = [[None] * 4, [None] * 4]
            E_trmm2 = [[None] * 4, [None] * 4]
            E_trevV = [None, None]
            E_trevA = [None, None]
            E_trevall = [[], []]
            E_msd = [None, None]
            oev = [None]
            trctr = [0]

            def sc_mms(h, c, j):
                par = h % 2
                qt = 4 * c + j
                kext = (qt + 1) * 128
                nkc = (kext + W - 1) // W
                chunks = []
                for kc in range(nkc):
                    wdt = min(W, kext - kc * W)
                    s = ring_i[0] % 5
                    ring_i[0] += 1
                    slot = ring_ap[s]
                    w0 = [ring_ev[s], E_qevac[h][c], E_kevac[NCH - 1],
                          E_Q_all]
                    diag_chunk = (kc == nkc - 1)
                    p.op("tensor", lambda e, slot=slot, h=h, qt=qt, kc=kc,
                         wdt=wdt, dgc=diag_chunk: e.matmul(
                        slot[:, 0:wdt],
                        QTO[:, h * S + qt * 128: h * S + qt * 128 + 128],
                        KT[:, kc * W: kc * W + wdt],
                        start=True, stop=(not dgc), skip_group_check=True),
                        waits=w0,
                        inc=(None if diag_chunk else "t"))
                    if diag_chunk:
                        E_mm = p.op("tensor", lambda e, slot=slot, wdt=wdt:
                                    e.matmul(
                            slot[:, wdt - 128:wdt], ident_s[:], trib_s[:],
                            start=False, stop=True, skip_group_check=True),
                            waits=[E_trib], inc="t")
                    else:
                        E_mm = ("t", p.counts["t"])
                    chunks.append((s, slot, wdt, kc, E_mm))
                return chunks

            def sc_soft(h, c, j, chunks):
                par = h % 2
                qt = 4 * c + j
                kext = (qt + 1) * 128
                nkc = (kext + W - 1) // W
                mcol = par * 16 + j * 4
                ncol = par * 4 + j
                mx_all = []
                for s, slot, wdt, kc, E_mm in chunks:
                    mx_all.append(p.op("vector", lambda e, slot=slot, wdt=wdt,
                                mc=mcol + kc: e.reduce_max(
                        mpart[:, mc:mc + 1], slot[:, 0:wdt], axis=AX.X),
                        waits=[E_mm, E_comb2[par][j]], inc="v"))
                E_comb2[par][j] = p.op("vector", lambda e, mcol=mcol,
                                       nkc=nkc, ncol=ncol: e.reduce_max(
                    negm8[:, ncol:ncol + 1], mpart[:, mcol:mcol + nkc],
                    axis=AX.X, negate=True),
                    waits=mx_all + E_exall2[par][j], inc="v")
                E_ex = None
                ex_all = []
                for s, slot, wdt, kc, E_mm in reversed(chunks):
                    if nkc == 1:
                        acc = den8[:, ncol:ncol + 1]
                        wx = [E_comb2[par][j], E_trmm2[par][j],
                              E_diag2v[par][j]]
                    else:
                        acc = dpart[:, mcol + kc:mcol + kc + 1]
                        wx = [E_comb2[par][j], E_trmm2[par][j],
                              E_dsum2[par][j]]
                    E_ex = p.op("scalar", lambda e, slot=slot, wdt=wdt,
                                kc=kc, ncol=ncol, acc=acc, par=par, qt=qt:
                                e.activation(
                        expS8[:, (par * 4 + qt % 4) * S + kc * W:
                              (par * 4 + qt % 4) * S + kc * W + wdt],
                        slot[:, 0:wdt], AF.Exp,
                        bias=negm8[:, ncol:ncol + 1], scale=1.0,
                        accum_out=acc), waits=wx, inc="a")
                    ring_ev[s] = E_ex
                    ex_all.append(E_ex)
                E_expl2[par][j] = E_ex
                E_exall2[par][j] = list(ex_all)
                if nkc > 1:
                    E_dsum2[par][j] = p.op("vector", lambda e, mcol=mcol,
                                           nkc=nkc, ncol=ncol: e.reduce_sum(
                        den8[:, ncol:ncol + 1], dpart[:, mcol:mcol + nkc],
                        axis=AX.X),
                        waits=ex_all + [E_diag2v[par][j]], inc="v")
                E_den = E_dsum2[par][j] if nkc > 1 else E_expl2[par][j]
                E_rq = p.op("vector", lambda e, ncol=ncol: e.reciprocal(
                    rcp8[:, ncol:ncol + 1], den8[:, ncol:ncol + 1]),
                    waits=[E_den, E_diag2v[par][j]], inc="v")
                E_diag2v[par][j] = p.op("scalar", lambda e, par=par, j=j,
                                        ncol=ncol: e.activation(
                    diagP[:, (par * 4 + j) * 128:(par * 4 + j + 1) * 128],
                    ident_s[:], AF.Identity,
                    bias=0.0, scale=rcp8[:, ncol:ncol + 1]),
                    waits=[E_rq, E_trmm2[par][j]], inc="a")

            def tr_unit(h, c, j):
                par = h % 2
                qt = 4 * c + j
                base = par * NT * W
                if j == 0:
                    E_trevall[par] = []
                last_mm = None
                for g in range(c + 1):
                    cnt = min(4, qt + 1 - 4 * g)

                    def mmTR(bk, w, g=g, cnt=cnt, par=par, qt=qt, j=j):
                        w = w + [E_diag2v[par][j]] + E_exall2[par][j]
                        for i in range(cnt):
                            kt = 4 * g + i
                            ev = p.op("tensor",
                                      lambda e, bk=bk, i=i, kt=kt, par=par,
                                      qt=qt: e.matmul(
                                bk[:, i * 128:(i + 1) * 128],
                                expS8[:, (par * 4 + qt % 4) * S + kt * 128:
                                      (par * 4 + qt % 4) * S +
                                      (kt + 1) * 128],
                                diagP[:, (par * 4 + qt % 4) * 128:
                                      (par * 4 + qt % 4 + 1) * 128],
                                start=True, stop=True),
                                waits=(w if i == 0 else ()),
                                inc=("t" if i == cnt - 1 else None))
                        return ev

                    def evTR(bk, w, g=g, cnt=cnt, j=j, base=base):
                        dst = expST2[:, base:base + NT * W].rearrange(
                            "p (t w) -> p t w", t=NT)[
                            :, 4 * g:4 * g + cnt, j * 128:(j + 1) * 128]
                        srcv = bk[:, 0:cnt * 128].rearrange(
                            "p (t w) -> p t w", t=cnt)
                        eng = "vector" if trctr[0] % 2 == 0 else "scalar"
                        trctr[0] += 1
                        if eng == "vector":
                            ev = p.op("vector", lambda e, d=dst, s=srcv:
                                      e.tensor_copy(d, s), waits=w, inc="v")
                            E_trevV[par] = ev
                        else:
                            ev = p.op("scalar", lambda e, d=dst, s=srcv:
                                      e.copy(d, s), waits=w, inc="a")
                            E_trevA[par] = ev
                        E_trevall[par].append(ev)
                        return ev
                    e_mm, _ = use_bank(mmTR, evTR)
                    last_mm = e_mm
                E_trmm2[par][j] = last_mm

            def av_unit(h, c, pj):
                par = h % 2
                nkt = 4 * c + 4
                base = par * NT * W
                w0 = []
                if pj == 0:
                    w0 = list(E_trevall[par]) + [E_msd[par], oev[0]]
                for i in range(4):
                    kt = 4 * pj + i
                    E_av = p.op("tensor", lambda e, kt=kt, nkt=nkt,
                                base=base: e.matmul(
                        psO[:],
                        Vsb[:, kt * 128:(kt + 1) * 128],
                        expST2[:, base + kt * W: base + (kt + 1) * W],
                        start=(kt == 0), stop=(kt == nkt - 1),
                        skip_group_check=True),
                        waits=(w0 if i == 0 else ()),
                        inc=("t" if kt == nkt - 1 else None))
                if pj == c:
                    oev[0] = p.op("scalar", lambda e, h=h, c=c: e.copy(
                        QTO[:, h * S + c * W: h * S + c * W + W], psO[:]),
                        waits=[E_av], inc="a")

            for c in range(NCH):
                for sl in range(2):
                    ev = None
                    for kt in range(4 * c + 1, 4 * c + 4):
                        for jj in range(kt - 4 * c):
                            ev = p.op("gpsimd", lambda e, sl=sl, kt=kt,
                                      jj=jj: e.memset(
                                expST2[:, sl * NT * W + kt * W + jj * 128:
                                       sl * NT * W + kt * W +
                                       (jj + 1) * 128], 0.0),
                                waits=[E_Q_all], inc="p")
                    E_msd[sl] = ev
                for step in range(H + 2):
                    for j in range(4):
                        if 1 <= step <= H:
                            tr_unit(step - 1, c, j)
                        if step >= 2 and j <= c:
                            av_unit(step - 2, c, j)
                        if step < H:
                            sc_soft(step, c, j, sc_mms(step, c, j))
            E_attn_done = oev[0]

            E_woe = [[None] * NT for _ in range(NCH)]
            E_fwe = [[None] * NT for _ in range(NCH)]
            E_xre = [[None] * NT for _ in range(NCH)]
            for ot in range(2):
                E_woe[0][ot] = p.op("sync", lambda e, ot=ot: e.dma_start(
                    out=wost[:, (ot % 2) * D:(ot % 2) * D + D],
                    in_=wo_bf[ot * 128:(ot + 1) * 128, :]),
                    waits=[E_Q_all, E_wob], inc="wo%d" % (ot % 2))
            xT4 = xT_h.rearrange("(t p) s -> t p s", p=128)
            for ot in range(3):
                E_xre[0][ot] = p.op("scalar", lambda e, ot=ot: e.dma_start(
                    out=xres[:, (ot % 3) * W:(ot % 3) * W + W],
                    in_=xT4[ot, :, 0:W]),
                    waits=[E_Q_all], inc="xr%d" % (ot % 3))

            stB2.close()
            stAB.close()
            stC = ExitStack()
            with stC:
                ycs = [stC.enter_context(
                           nc.sbuf_tensor("ycs0", [128, NT * W], F32)),
                       stC.enter_context(
                           nc.sbuf_tensor("ycs1", [128, NT * W], F32))]
                x1T = stC.enter_context(
                    nc.sbuf_tensor("x1T", [128, NT * W], BF16))
                castsc = stC.enter_context(
                    nc.sbuf_tensor("castsc", [128, 16 * W], BF16))
                fwst = stC.enter_context(
                    nc.sbuf_tensor("fwst", [128, 2 * D], BF16))
                yscr = stC.enter_context(
                    nc.sbuf_tensor("yscr", [128, 3 * W], F32))
                zscr = stC.enter_context(
                    nc.sbuf_tensor("zscr", [128, 3 * W], F32))

                pool_banks[:] = [("psT0", psT0), ("psT1", psT1),
                                 ("psU", psU), ("psO", psO)]
                bank_evac["psO"] = E_attn_done

                OROT, OD = 5, 4
                FROT, FD = 3, 2
                OY, OSQ, FY, FSQ = 0, 5, 10, 13
                BCA, BCB = 1024, 1536
                XC, YC = 0, 512

                E_Omm = [[None] * NT for _ in range(NCH)]
                E_Fmm = [[None] * NT for _ in range(NCH)]
                E_evO = [[None] * NT for _ in range(NCH)]
                E_evF = [[None] * NT for _ in range(NCH)]
                E_addO = [[None] * NT for _ in range(NCH)]
                E_zadd = [[None] * NT for _ in range(NCH)]
                E_castO = [[None] * NT for _ in range(NCH)]
                E_sqO = [[None] * NT for _ in range(NCH)]
                E_castF = [[None] * NT for _ in range(NCH)]
                E_sqF = [[None] * NT for _ in range(NCH)]
                E_statO = [[None] * NT for _ in range(NCH)]
                E_statF = [[None] * NT for _ in range(NCH)]
                E_x1b = [[None] * NT for _ in range(NCH)]
                E_x1t = [[None] * NT for _ in range(NCH)]
                E_outa = [[None] * NT for _ in range(NCH)]
                E_lnrdO = [None] * NCH
                E_lnbO = [None] * NCH
                E_lnrdF = [None] * NCH
                E_lnbF = [None] * NCH
                E_abc1 = [None] * NCH
                E_abc2 = [None] * NCH
                last_bcb = [None]

                def prev_slot_user(ot, rot):
                    return ot + rot * ((NT - 1 - ot) // rot)

                def emit_stats(kind, c, j):
                    if kind == "O":
                        rp, ysl, qsl = 0, OY + j % OROT, OSQ + j % OROT
                        Ec, Es = E_castO[c][j], E_sqO[c][j]
                        w0 = []
                        if j == 0:
                            w0 = ([E_attn_done] if c == 0
                                  else [E_lnrdO[c - 1]])
                    else:
                        rp, ysl, qsl = 32, FY + j % FROT, FSQ + j % FROT
                        Ec, Es = E_castF[c][j], E_sqF[c][j]
                        w0 = []
                        if j == 0:
                            w0 = ([E_attn_done] if c == 0
                                  else [E_lnrdF[c - 1]])
                    p.op("tensor", lambda e, rp=rp, ysl=ysl: e.matmul(
                        psS[rp:rp + 1, 0:W], onesc_s[:],
                        castsc[:, ysl * W:(ysl + 1) * W],
                        start=(j == 0), stop=(j == NT - 1),
                        skip_group_check=True), waits=[Ec] + w0)
                    ev = p.op("tensor", lambda e, rp=rp, qsl=qsl: e.matmul(
                        psS[rp:rp + 1, W:2 * W], onesc_s[:],
                        castsc[:, qsl * W:(qsl + 1) * W],
                        start=(j == 0), stop=(j == NT - 1),
                        skip_group_check=True), waits=[Es] + w0, inc="t")
                    if kind == "O":
                        E_statO[c][j] = ev
                    else:
                        E_statF[c][j] = ev

                def emit_lnfin(kind, c):
                    rp = 0 if kind == "O" else 32
                    Estat = (E_statO if kind == "O" else E_statF)[c][NT - 1]
                    w0 = [Estat]
                    if last_bcb[0] is not None:
                        w0.append(last_bcb[0])
                    rX = rowb[rp:rp + 1, XC:XC + W]
                    rY = rowb[rp:rp + 1, YC:YC + W]
                    p1 = psS[rp:rp + 1, 0:W]
                    p2 = psS[rp:rp + 1, W:2 * W]
                    e1 = p.op("vector", lambda e: e.tensor_scalar(
                        rX, p1, 1.0 / 2048.0, None, AluOpType.mult),
                        waits=w0, inc="v")
                    e2 = p.op("vector", lambda e: e.tensor_scalar(
                        rY, p2, 1.0 / 2048.0, None, AluOpType.mult),
                        waits=[e1], inc="v")
                    e3 = p.op("vector", lambda e: e.tensor_tensor(
                        p1, rX, rX, AluOpType.mult), waits=[e2], inc="v")
                    e4 = p.op("vector", lambda e: e.tensor_tensor(
                        rY, rY, p1, AluOpType.subtract), waits=[e3], inc="v")
                    e5 = p.op("scalar", lambda e: e.activation(
                        p2, rY, AF.Ln, bias=eps1[rp:rp + 1, :],
                        scale=1.0), waits=[e4], inc="a")
                    e6 = p.op("scalar", lambda e: e.activation(
                        rY, p2, AF.Exp, bias=0.0, scale=-0.5),
                        waits=[e5], inc="a")
                    e7 = p.op("vector", lambda e: e.tensor_tensor(
                        rX, rX, rY, AluOpType.mult), waits=[e6], inc="v")
                    eb = p.op("vector", lambda e: e.tensor_scalar(
                        rX, rX, -1.0, None, AluOpType.mult),
                        waits=[e7], inc="v")
                    if kind == "O":
                        E_lnrdO[c], E_lnbO[c] = e6, eb
                    else:
                        E_lnrdF[c], E_lnbF[c] = e6, eb

                def emit_bcast_a(kind, c):
                    rp = 0 if kind == "O" else 32
                    Eb = (E_lnbO if kind == "O" else E_lnbF)[c]
                    w = [Eb]
                    if kind == "O":
                        w.append(E_abc2[c - 1] if c > 0 else E_attn_done)
                    else:
                        w.append(E_abc1[c])
                    p.op("tensor", lambda e, rp=rp: e.matmul(
                        psS[:, BCA:BCA + W], onesrF_s[rp:rp + 1, :],
                        rowb[rp:rp + 1, YC:YC + W], start=True, stop=True),
                        waits=w, inc="t")

                def emit_bcast_b(kind, c):
                    rp = 0 if kind == "O" else 32
                    ev = p.op("tensor", lambda e, rp=rp: e.matmul(
                        psS[:, BCB:BCB + W], onesrF_s[rp:rp + 1, :],
                        rowb[rp:rp + 1, XC:XC + W], start=True, stop=True),
                        inc="t")
                    last_bcb[0] = ev
                    return ev

                def emit_abc(kind, c, Ebcb):
                    if kind == "O":
                        at, bt = abc1_s, bbc1_s
                        war = E_x1b[c - 1][NT - 1] if c > 0 else None
                    else:
                        at, bt = abc2_s, bbc2_s
                        war = E_outa[c - 1][NT - 1] if c > 0 else None
                    p.op("vector", lambda e, at=at: e.tensor_copy(
                        at[:], psS[:, BCA:BCA + W]), waits=[Ebcb, war])
                    ev = p.op("vector", lambda e, bt=bt: e.tensor_copy(
                        bt[:], psS[:, BCB:BCB + W]), inc="v")
                    if kind == "O":
                        E_abc1[c] = ev
                    else:
                        E_abc2[c] = ev

                def emit_x1(c, ots):
                    pbuf = ycs[c % 2]
                    for ot in ots:
                        em = p.op("vector", lambda e, ot=ot, pbuf=pbuf:
                                  e.tensor_tensor(
                            pbuf[:, ot * W:(ot + 1) * W],
                            pbuf[:, ot * W:(ot + 1) * W], abc1_s[:],
                            AluOpType.mult),
                            waits=[E_abc1[c], E_castO[c][ot], E_sqO[c][ot]],
                            inc="v")
                        E_x1b[c][ot] = p.op("vector", lambda e, ot=ot,
                                            pbuf=pbuf: e.tensor_tensor(
                            pbuf[:, ot * W:(ot + 1) * W],
                            pbuf[:, ot * W:(ot + 1) * W], bbc1_s[:],
                            AluOpType.add), waits=[em], inc="v")

                def emit_x1t(c):
                    pbuf = ycs[c % 2]
                    wf = [E_Fmm[c - 1][NT - 1]] if c > 0 else []
                    for ot in range(NT):
                        E_x1t[c][ot] = p.op("scalar", lambda e, ot=ot,
                                            pbuf=pbuf: e.copy(
                            x1T[:, ot * W:(ot + 1) * W],
                            pbuf[:, ot * W:(ot + 1) * W]),
                            waits=[E_x1b[c][ot]] + wf, inc="a")

                def emit_out(c, ots):
                    pbuf = ycs[c % 2]
                    for ot in ots:
                        em = p.op("vector", lambda e, ot=ot, pbuf=pbuf:
                                  e.tensor_tensor(
                            pbuf[:, ot * W:(ot + 1) * W],
                            pbuf[:, ot * W:(ot + 1) * W], abc2_s[:],
                            AluOpType.mult),
                            waits=[E_abc2[c], E_castF[c][ot], E_sqF[c][ot]],
                            inc="v")
                        E_outa[c][ot] = p.op("vector", lambda e, ot=ot,
                                             pbuf=pbuf: e.tensor_tensor(
                            pbuf[:, ot * W:(ot + 1) * W],
                            pbuf[:, ot * W:(ot + 1) * W], bbc2_s[:],
                            AluOpType.add), waits=[em], inc="v")
                        wod = [E_outa[c][ot]]
                        if ot == 0:
                            wod.append(("od", 256 * c) if c > 0 else None)
                        p.op("sync", lambda e, c=c, ot=ot, pbuf=pbuf:
                             e.dma_start(
                            out=out_h[ot * 128:(ot + 1) * 128,
                                      c * W:(c + 1) * W],
                            in_=pbuf[:, ot * W:(ot + 1) * W]),
                            waits=wod, inc="od")

                def o_group(c, ot):
                    pbuf = ycs[c % 2]
                    if ot >= 2:
                        E_woe[c][ot] = p.op("sync", lambda e, c=c, ot=ot:
                                            e.dma_start(
                            out=wost[:, (ot % 2) * D:(ot % 2) * D + D],
                            in_=wo_bf[ot * 128:(ot + 1) * 128, :]),
                            waits=[E_Omm[c][ot - 2], E_wob],
                            inc="wo%d" % (ot % 2))
                    if ot >= 3:
                        E_xre[c][ot] = p.op("scalar", lambda e, c=c, ot=ot:
                                            e.dma_start(
                            out=xres[:, (ot % 3) * W:(ot % 3) * W + W],
                            in_=xT4[ot, :, c * W:(c + 1) * W]),
                            waits=[E_addO[c][ot - 3]], inc="xr%d" % (ot % 3))

                    def mmO(bk, w, c=c, ot=ot):
                        w = w + [E_woe[c][ot], E_attn_done]
                        for ht in range(NT):
                            ev = p.op("tensor", lambda e, bk=bk, ht=ht, c=c,
                                      ot=ot: e.matmul(
                                bk[:],
                                wost[:, (ot % 2) * D + ht * 128:
                                     (ot % 2) * D + ht * 128 + 128],
                                QTO[:, ht * S + c * W: ht * S + c * W + W],
                                start=(ht == 0), stop=(ht == NT - 1)),
                                waits=(w if ht == 0 else ()),
                                inc=("t" if ht == NT - 1 else None))
                        return ev

                    def evO(bk, w, c=c, ot=ot):
                        if ot >= 3:
                            w = w + [E_addO[c][ot - 3]]
                        elif c > 0:
                            w = w + [E_addO[c - 1][prev_slot_user(ot, 3)]]
                        return p.op("scalar", lambda e, bk=bk, c=c, ot=ot:
                                    e.activation(
                            yscr[:, (ot % 3) * W:(ot % 3) * W + W],
                            bk[:], AF.Identity, bias=ob_s[:, ot:ot + 1],
                            scale=1.0), waits=w, inc="a")
                    e_mm, e_ev = use_bank(mmO, evO)
                    E_Omm[c][ot] = e_mm
                    E_evO[c][ot] = e_ev
                    wadd = [e_ev, E_xre[c][ot]]
                    if c >= 2:
                        wadd.append(("od", 256 * (c - 1)))
                    E_addO[c][ot] = p.op("gpsimd", lambda e, c=c, ot=ot,
                                         pbuf=pbuf: e.tensor_tensor(
                        pbuf[:, ot * W:(ot + 1) * W],
                        yscr[:, (ot % 3) * W:(ot % 3) * W + W],
                        xres[:, (ot % 3) * W:(ot % 3) * W + W],
                        AluOpType.add), waits=wadd, inc="p")
                    wcs = [E_addO[c][ot]]
                    if ot >= OROT:
                        wcs.append(E_statO[c][ot - OROT])
                    elif c > 0:
                        wcs.append(E_statO[c - 1][prev_slot_user(ot, OROT)])
                    ysl = OY + ot % OROT
                    qsl = OSQ + ot % OROT
                    E_castO[c][ot] = p.op("gpsimd", lambda e, ot=ot, ysl=ysl,
                                          pbuf=pbuf: e.tensor_copy(
                        castsc[:, ysl * W:(ysl + 1) * W],
                        pbuf[:, ot * W:(ot + 1) * W]), waits=wcs, inc="p")
                    E_sqO[c][ot] = p.op("gpsimd", lambda e, ot=ot, qsl=qsl,
                                        pbuf=pbuf: e.tensor_tensor(
                        castsc[:, qsl * W:(qsl + 1) * W],
                        pbuf[:, ot * W:(ot + 1) * W],
                        pbuf[:, ot * W:(ot + 1) * W], AluOpType.mult),
                        waits=wcs, inc="p")
                    if ot >= OD:
                        emit_stats("O", c, ot - OD)

                def f_group(c, ot):
                    pbuf = ycs[c % 2]
                    if ot >= 2:
                        E_fwe[c][ot] = p.op("sync", lambda e, c=c, ot=ot:
                                            e.dma_start(
                            out=fwst[:, (ot % 2) * D:(ot % 2) * D + D],
                            in_=fw_bf[ot * 128:(ot + 1) * 128, :]),
                            waits=[E_Fmm[c][ot - 2], E_fwb],
                            inc="fw%d" % (ot % 2))

                    def mmF(bk, w, c=c, ot=ot):
                        w = w + [E_fwe[c][ot], E_x1t[c][NT - 1]]
                        for dt in range(NT):
                            ev = p.op("tensor", lambda e, bk=bk, dt=dt, c=c,
                                      ot=ot: e.matmul(
                                bk[:],
                                fwst[:, (ot % 2) * D + dt * 128:
                                     (ot % 2) * D + dt * 128 + 128],
                                x1T[:, dt * W:(dt + 1) * W],
                                start=(dt == 0), stop=(dt == NT - 1)),
                                waits=(w if dt == 0 else ()),
                                inc=("t" if dt == NT - 1 else None))
                        return ev

                    def evF(bk, w, c=c, ot=ot):
                        if ot >= 3:
                            w = w + [E_zadd[c][ot - 3]]
                        elif c > 0:
                            w = w + [E_zadd[c - 1][prev_slot_user(ot, 3)]]
                        return p.op("scalar", lambda e, bk=bk, c=c, ot=ot:
                                    e.activation(
                            zscr[:, (ot % 3) * W:(ot % 3) * W + W],
                            bk[:], AF.Identity, bias=fb_s[:, ot:ot + 1],
                            scale=fsc_s[:, ot:ot + 1]), waits=w, inc="a")
                    e_mm, e_ev = use_bank(mmF, evF)
                    E_Fmm[c][ot] = e_mm
                    E_evF[c][ot] = e_ev
                    E_zadd[c][ot] = p.op("gpsimd", lambda e, c=c, ot=ot,
                                         pbuf=pbuf: e.tensor_tensor(
                        pbuf[:, ot * W:(ot + 1) * W],
                        zscr[:, (ot % 3) * W:(ot % 3) * W + W],
                        pbuf[:, ot * W:(ot + 1) * W], AluOpType.add),
                        waits=[e_ev, E_x1t[c][ot]], inc="p")
                    wcs = [E_zadd[c][ot]]
                    if ot >= FROT:
                        wcs.append(E_statF[c][ot - FROT])
                    elif c > 0:
                        wcs.append(E_statF[c - 1][prev_slot_user(ot, FROT)])
                    ysl = FY + ot % FROT
                    qsl = FSQ + ot % FROT
                    E_castF[c][ot] = p.op("scalar", lambda e, ot=ot, ysl=ysl,
                                          pbuf=pbuf: e.copy(
                        castsc[:, ysl * W:(ysl + 1) * W],
                        pbuf[:, ot * W:(ot + 1) * W]), waits=wcs, inc="a")
                    E_sqF[c][ot] = p.op("gpsimd", lambda e, ot=ot, qsl=qsl,
                                        pbuf=pbuf: e.tensor_tensor(
                        castsc[:, qsl * W:(qsl + 1) * W],
                        pbuf[:, ot * W:(ot + 1) * W],
                        pbuf[:, ot * W:(ot + 1) * W], AluOpType.mult),
                        waits=wcs, inc="p")
                    if ot >= FD:
                        emit_stats("F", c, ot - FD)

                def make_tail(kind, c):
                    if kind == "O":
                        return [
                            lambda: (emit_stats("O", c, 12),
                                     emit_stats("O", c, 13),
                                     emit_stats("O", c, 14),
                                     emit_stats("O", c, 15)),
                            lambda: emit_lnfin("O", c),
                            lambda: emit_bcast_a("O", c),
                            lambda: emit_abc("O", c, emit_bcast_b("O", c)),
                            lambda: emit_x1(c, range(0, 4)),
                            lambda: emit_x1(c, range(4, 8)),
                            lambda: emit_x1(c, range(8, 12)),
                            lambda: emit_x1(c, range(12, 16)),
                        ], (lambda: emit_x1t(c))
                    else:
                        return [
                            lambda: (emit_stats("F", c, 14),
                                     emit_stats("F", c, 15)),
                            lambda: emit_lnfin("F", c),
                            lambda: emit_bcast_a("F", c),
                            lambda: emit_abc("F", c, emit_bcast_b("F", c)),
                            lambda: emit_out(c, range(0, 4)),
                            lambda: emit_out(c, range(4, 8)),
                            lambda: emit_out(c, range(8, 12)),
                            lambda: emit_out(c, range(12, 16)),
                        ], None

                def preload(kind, c):
                    if kind == "O":
                        for ot in range(2):
                            E_woe[c][ot] = p.op(
                                "sync", lambda e, c=c, ot=ot: e.dma_start(
                                    out=wost[:, (ot % 2) * D:
                                             (ot % 2) * D + D],
                                    in_=wo_bf[ot * 128:(ot + 1) * 128, :]),
                                waits=[E_Omm[c - 1][NT - 2 + ot], E_wob],
                                inc="wo%d" % (ot % 2))
                        for ot in range(3):
                            E_xre[c][ot] = p.op(
                                "scalar", lambda e, c=c, ot=ot: e.dma_start(
                                    out=xres[:, (ot % 3) * W:
                                             (ot % 3) * W + W],
                                    in_=xT4[ot, :, c * W:(c + 1) * W]),
                                waits=[E_addO[c - 1][prev_slot_user(ot, 3)]],
                                inc="xr%d" % (ot % 3))
                    else:
                        for ot in range(2):
                            wf = ([E_Fmm[c - 1][NT - 2 + ot]] if c > 0
                                  else [E_attn_done])
                            E_fwe[c][ot] = p.op(
                                "sync", lambda e, c=c, ot=ot: e.dma_start(
                                    out=fwst[:, (ot % 2) * D:
                                             (ot % 2) * D + D],
                                    in_=fw_bf[ot * 128:(ot + 1) * 128, :]),
                                waits=wf + [E_fwb], inc="fw%d" % (ot % 2))

                blocks = [("O", 0), ("O", 1), ("F", 0), ("O", 2),
                          ("F", 1), ("O", 3), ("F", 2), ("F", 3)]
                pend, pend_end = [], None
                for bi, (kind, c) in enumerate(blocks):
                    grp = o_group if kind == "O" else f_group
                    for ot in range(NT):
                        if ot < len(pend):
                            pend[ot]()
                        grp(c, ot)
                    for k in range(NT, len(pend)):
                        pend[k]()
                    if pend_end is not None:
                        pend_end()
                    if bi + 1 < len(blocks):
                        preload(*blocks[bi + 1])
                    pend, pend_end = make_tail(kind, c)
                for k in range(len(pend)):
                    pend[k]()

                p.op("gpsimd",
                     lambda e: e.wait_ge(sems["od"], 16 * NT * NCH))

        nc._prog = p
        with nc.Block() as block:
            block.sync(lambda e: p.emit("sync", e, sems))
            block.tensor(lambda e: p.emit("tensor", e, sems))
            block.vector(lambda e: p.emit("vector", e, sems))
            block.scalar(lambda e: p.emit("scalar", e, sems))
            block.gpsimd(lambda e: p.emit("gpsimd", e, sems))
    return nc


def stage_sq(w):
    w = np.asarray(w, np.float32)
    return np.ascontiguousarray(
        w.reshape(NT, 128, NT, 128).transpose(0, 3, 2, 1)).reshape(D, D)


def stage_kv(w):
    w = np.asarray(w, np.float32)
    return np.ascontiguousarray(
        w.reshape(128, NT, 128).transpose(2, 1, 0)).reshape(128, D)


def kernel(L0_attn_hidden_states, L0_attn_q_w, L0_attn_q_b, L0_attn_k_w,
           L0_attn_k_b, L0_attn_v_w, L0_attn_v_b, L0_attn_o_w, L0_attn_o_b,
           L0_ffn_weight, L0_ffn_scale_weight, L0_ffn_bias):
    x = np.asarray(L0_attn_hidden_states, np.float32)
    B = x.shape[0]

    qw = stage_sq(L0_attn_q_w)
    kw = stage_kv(L0_attn_k_w)
    vw = stage_kv(L0_attn_v_w)
    wo = stage_sq(L0_attn_o_w)
    fw = stage_sq(L0_ffn_weight)

    def btile(b):
        return np.ascontiguousarray(
            np.asarray(b, np.float32).reshape(NT, 128).T)

    qb = btile(L0_attn_q_b)
    ob = btile(L0_attn_o_b)
    fb = btile(L0_ffn_bias)
    fsc = btile(L0_ffn_scale_weight)
    kb = np.ascontiguousarray(np.asarray(L0_attn_k_b, np.float32)[:, None])
    vb = np.ascontiguousarray(np.asarray(L0_attn_v_b, np.float32)[:, None])

    qi = np.arange(128)
    tri = np.where(qi[None, :] <= qi[:, None], 0.0, NEG_BIG).astype(BF)
    ident = np.eye(128, dtype=BF)
    onesc = np.ones((128, 1), dtype=BF)
    onesrF = np.ones((33, 128), np.float32)

    nc = build_program()
    common = dict(qw=qw, kw=kw, vw=vw, wo=wo, fw=fw, qb=qb, ob=ob, fb=fb,
                  fsc=fsc, kb=kb, vb=vb, tri=tri, ident=ident, onesc=onesc,
                  onesrF=onesrF)
    in_maps = [dict(common, xT=np.ascontiguousarray(x[b].T)) for b in range(B)]
    res = run_bass_kernel_spmd(nc, in_maps, core_ids=list(range(B)))
    out = np.stack([np.asarray(r["outT"], np.float32).T
                    for r in res.results], axis=0)
    return np.ascontiguousarray(out)


# version 19
# speedup vs baseline: 1.1404x; 1.1404x over previous
import numpy as np
import ml_dtypes
from contextlib import ExitStack

import concourse.bass as bass
import concourse.mybir as mybir
from concourse.bass_utils import run_bass_kernel_spmd
from concourse.alu_op_type import AluOpType

F32 = mybir.dt.float32
BF16 = mybir.dt.bfloat16
AF = mybir.ActivationFunctionType
AX = mybir.AxisListType
BF = ml_dtypes.bfloat16

D = 2048
S = 2048
H = 16
DK = 128
NT = 16
NCH = 4
W = 512
INV_SQRT_DK = 1.0 / (DK ** 0.5)
NEG_BIG = -1.0e30
EPS = 1e-5

SEMS = {"t": 1, "v": 1, "a": 1, "p": 1,
        "cst": 16, "xbf": 16, "kvw": 16,
        "qw0": 16, "qw1": 16, "wo0": 16, "wo1": 16,
        "fw0": 16, "fw1": 16, "xr0": 16, "xr1": 16, "xr2": 16,
        "wob": 16, "fwb": 16, "od": 16}


ENG_SEM = {"tensor": "t", "vector": "v", "scalar": "a", "gpsimd": "p",
           "sync": None}


class Prog:

    def __init__(self, serial=False):
        self.ops = []
        self.counts = {k: 0 for k in SEMS}
        self.serial = serial
        self.last = None

    def op(self, eng, fn, waits=(), inc=None):
        w = [e for e in waits if e is not None]
        if self.serial:
            if self.last is not None:
                w = w + [self.last]
            if inc is None and ENG_SEM[eng] is not None:
                inc = ENG_SEM[eng]
        ev = None
        if inc is not None:
            self.counts[inc] += SEMS[inc]
            ev = (inc, self.counts[inc])
        self.ops.append((eng, fn, w, inc))
        if self.serial and ev is not None:
            self.last = ev
        return ev

    def emit(self, eng_name, eng, sems):
        hi = {}
        for idx, (name, fn, waits, inc) in enumerate(self.ops):
            if name != eng_name:
                continue
            for s, v in waits:
                if hi.get(s, 0) < v:
                    eng.wait_ge(sems[s], v)
                    hi[s] = v
            ins = fn(eng)
            if inc is not None:
                if ins is None:
                    ins = eng.nop()
                ins.then_inc(sems[inc], SEMS[inc])


def build_program(serial=False):
    nc = bass.Bass("TRN2", target_bir_lowering=False)

    def dram(name, shape, dt, kind="ExternalInput"):
        return nc.dram_tensor(name, shape, dt, kind=kind).ap()

    xT_h = dram("xT", [D, S], F32)
    qw_h = dram("qw", [D, D], F32)
    kw_h = dram("kw", [128, D], F32)
    vw_h = dram("vw", [128, D], F32)
    wo_h = dram("wo", [D, D], F32)
    fw_h = dram("fw", [D, D], F32)
    qb_h = dram("qb", [128, NT], F32)
    ob_h = dram("ob", [128, NT], F32)
    fb_h = dram("fb", [128, NT], F32)
    fsc_h = dram("fsc", [128, NT], F32)
    kb_h = dram("kb", [128, 1], F32)
    vb_h = dram("vb", [128, 1], F32)
    tri_h = dram("tri", [128, 128], BF16)
    ident_h = dram("ident", [128, 128], BF16)
    onesc_h = dram("onesc", [128, 1], BF16)
    onesrF_h = dram("onesrF", [33, 128], F32)
    out_h = dram("outT", [D, S], F32, kind="ExternalOutput")
    wo_bf = dram("wo_bf", [D, D], BF16, kind="Internal")
    fw_bf = dram("fw_bf", [D, D], BF16, kind="Internal")

    p = Prog(serial=serial)
    st = ExitStack()
    sb = lambda n, s, d: st.enter_context(nc.sbuf_tensor(n, s, d))
    ps = lambda n, s, d: st.enter_context(nc.psum_tensor(n, s, d))

    with st:
        onesc_s = sb("onesc_s", [128, 1], BF16)
        onesrF_s = sb("onesrF_s", [33, 128], F32)
        qb_s = sb("qb_s", [128, NT], F32)
        qbs_s = sb("qbs_s", [128, NT], F32)
        ob_s = sb("ob_s", [128, NT], F32)
        fb_s = sb("fb_s", [128, NT], F32)
        fsc_s = sb("fsc_s", [128, NT], F32)
        kb_s = sb("kb_s", [128, 1], F32)
        vb_s = sb("vb_s", [128, 1], F32)
        rowb = sb("rowb", [33, 1024], F32)
        eps1 = sb("eps1", [33, 1], F32)
        abc1_s = sb("abc1_s", [128, W], F32)
        bbc1_s = sb("bbc1_s", [128, W], F32)
        abc2_s = sb("abc2_s", [128, W], F32)
        bbc2_s = sb("bbc2_s", [128, W], F32)
        wost = sb("wost", [128, 2 * D], BF16)
        xres = sb("xres", [128, 3 * W], F32)

        psS = ps("psS", [128, 2048], F32)
        psT0 = ps("psT0", [128, W], F32)
        psT1 = ps("psT1", [128, W], F32)
        psO = ps("psO", [128, W], F32)
        psU = ps("psU", [128, W], F32)
        psT = [psT0, psT1]

        sems = {k: st.enter_context(nc.semaphore(k)) for k in SEMS}

        pool_banks = [("psT0", psT0), ("psT1", psT1)]
        bank_evac = {"psT0": None, "psT1": None, "psO": None, "psU": None}
        rot_idx = [0]

        def use_bank(mm_emit, evac_emit):
            b = rot_idx[0] % len(pool_banks)
            rot_idx[0] += 1
            key, bk = pool_banks[b]
            mmw = [bank_evac[key]] if bank_evac[key] else []
            e_mm = mm_emit(bk, mmw)
            e_ev = evac_emit(bk, [e_mm])
            bank_evac[key] = e_ev
            return e_mm, e_ev

        with nc.sbuf_tensor("QTO", [128, NT * S], BF16) as QTO:
            stAB = ExitStack()
            ident_s = stAB.enter_context(
                nc.sbuf_tensor("ident_s", [128, 128], BF16))
            KT = stAB.enter_context(nc.sbuf_tensor("KT", [128, S], BF16))
            Vsb = stAB.enter_context(nc.sbuf_tensor("Vsb", [128, NT * 128], BF16))

            for dst, src in [(ident_s, ident_h),
                             (onesc_s, onesc_h), (onesrF_s, onesrF_h),
                             (qb_s, qb_h), (ob_s, ob_h), (fb_s, fb_h),
                             (fsc_s, fsc_h), (kb_s, kb_h), (vb_s, vb_h)]:
                E_consts = p.op("sync", lambda e, d=dst, s=src: e.dma_start(
                    out=d[:], in_=s), inc="cst")
            p.op("vector", lambda e: e.memset(eps1[:], EPS), inc="v")
            E_qbs = p.op("scalar", lambda e: e.mul(
                qbs_s[:], qb_s[:], INV_SQRT_DK), waits=[E_consts], inc="a")

            stA = ExitStack()
            with stA:
                xTbf = stA.enter_context(nc.sbuf_tensor("xTbf", [128, NT * S], BF16))
                qwst = stA.enter_context(nc.sbuf_tensor("qwst", [128, 2 * D], BF16))
                kwst = stA.enter_context(nc.sbuf_tensor("kwst", [128, NT * 128], BF16))
                vwst = stA.enter_context(nc.sbuf_tensor("vwst", [128, NT * 128], BF16))

                E_xbf = p.op("gpsimd", lambda e: e.dma_start(
                    out=xTbf[:].rearrange("p (t s) -> p t s", t=NT),
                    in_=xT_h.rearrange("(t p) s -> p t s", p=128)), inc="xbf")
                p.op("gpsimd", lambda e: e.dma_start(
                    out=kwst[:], in_=kw_h), inc="kvw")
                E_kvw = p.op("gpsimd", lambda e: e.dma_start(
                    out=vwst[:], in_=vw_h), inc="kvw")

                E_kevac = [None] * NCH
                for c in range(NCH):
                    def mmK(bk, w, c=c):
                        w = w + [E_xbf, E_kvw]
                        for dt in range(NT):
                            ev = p.op("tensor", lambda e, bk=bk, dt=dt, c=c: e.matmul(
                                bk[:],
                                kwst[:, dt * 128:(dt + 1) * 128],
                                xTbf[:, dt * S + c * W: dt * S + c * W + W],
                                start=(dt == 0), stop=(dt == NT - 1)),
                                waits=(w if dt == 0 else ()),
                                inc=("t" if dt == NT - 1 else None))
                        return ev

                    def evK(bk, w, c=c):
                        return p.op("scalar", lambda e, bk=bk, c=c: e.activation(
                            KT[:, c * W:(c + 1) * W], bk[:], AF.Identity,
                            bias=kb_s[:], scale=1.0),
                            waits=w + [E_consts], inc="a")
                    _, E_kevac[c] = use_bank(mmK, evK)

                E_vevac = [None] * NCH
                for c in range(NCH):
                    def mmV(bk, w, c=c):
                        w = w + [E_kvw]
                        for dt in range(NT):
                            ev = p.op("tensor", lambda e, bk=bk, dt=dt, c=c: e.matmul(
                                bk[:],
                                vwst[:, dt * 128:(dt + 1) * 128],
                                xTbf[:, dt * S + c * W: dt * S + c * W + W],
                                start=(dt == 0), stop=(dt == NT - 1)),
                                waits=(w if dt == 0 else ()),
                                inc=("t" if dt == NT - 1 else None))
                        return ev

                    def evV(bk, w, c=c):
                        return p.op("scalar", lambda e, bk=bk, c=c: e.activation(
                            QTO[:, 15 * S + c * W: 15 * S + (c + 1) * W],
                            bk[:], AF.Identity,
                            bias=vb_s[:], scale=1.0), waits=w, inc="a")
                    _, E_vevac[c] = use_bank(mmV, evV)

                E_vsb = None
                for g in range(4):
                    def mmVT(bk, w, g=g):
                        w = w + [E_vevac[g], E_consts]
                        for j in range(4):
                            kt = 4 * g + j
                            ev = p.op("tensor", lambda e, bk=bk, j=j, kt=kt: e.matmul(
                                bk[:, j * 128:(j + 1) * 128],
                                QTO[:, 15 * S + kt * 128:
                                    15 * S + (kt + 1) * 128],
                                ident_s[:], start=True, stop=True),
                                waits=(w if j == 0 else ()),
                                inc=("t" if j == 3 else None))
                        return ev

                    def evVT(bk, w, g=g):
                        return p.op("scalar", lambda e, bk=bk, g=g: e.copy(
                            Vsb[:, g * W:(g + 1) * W], bk[:]), waits=w, inc="a")
                    _, E_vsb = use_bank(mmVT, evVT)

                E_qw = [None] * NT
                E_qmm_last = [None] * NT
                E_qevac = [[None] * NCH for _ in range(NT)]
                for ot in range(NT):
                    wq = [E_qmm_last[ot - 2]] if ot >= 2 else []
                    E_qw[ot] = p.op("gpsimd", lambda e, ot=ot: e.dma_start(
                        out=qwst[:, (ot % 2) * D:(ot % 2) * D + D],
                        in_=qw_h[ot * 128:(ot + 1) * 128, :]),
                        waits=wq, inc="qw%d" % (ot % 2))
                    for c in range(NCH):
                        def mmQ(bk, w, ot=ot, c=c):
                            w = w + [E_qw[ot]]
                            for dt in range(NT):
                                ev = p.op("tensor", lambda e, bk=bk, dt=dt, ot=ot, c=c:
                                          e.matmul(
                                    bk[:],
                                    qwst[:, (ot % 2) * D + dt * 128:
                                         (ot % 2) * D + dt * 128 + 128],
                                    xTbf[:, dt * S + c * W: dt * S + c * W + W],
                                    start=(dt == 0), stop=(dt == NT - 1)),
                                    waits=(w if dt == 0 else ()),
                                    inc=("t" if dt == NT - 1 else None))
                            return ev

                        def evQ(bk, w, ot=ot, c=c):
                            if ot == 15:
                                w = w + [E_vsb]
                            return p.op("scalar", lambda e, bk=bk, ot=ot, c=c:
                                        e.activation(
                                QTO[:, ot * S + c * W: ot * S + c * W + W],
                                bk[:], AF.Identity,
                                bias=qbs_s[:, ot:ot + 1], scale=INV_SQRT_DK),
                                waits=w + [E_qbs], inc="a")
                        e_mm, e_ev = use_bank(mmQ, evQ)
                        E_qevac[ot][c] = e_ev
                        if c == NCH - 1:
                            E_qmm_last[ot] = e_mm
                E_Q_all = E_qmm_last[NT - 1]

            for q in range(4):
                p.op("gpsimd", lambda e, q=q: e.dma_start(
                    out=wo_bf[q * 512:(q + 1) * 512, :],
                    in_=wo_h[q * 512:(q + 1) * 512, :]), inc="wob")
                p.op("gpsimd", lambda e, q=q: e.dma_start(
                    out=fw_bf[q * 512:(q + 1) * 512, :],
                    in_=fw_h[q * 512:(q + 1) * 512, :]), inc="fwb")
            E_wob = ("wob", 64)
            E_fwb = ("fwb", 64)

            stB2 = ExitStack()
            trib_s = stB2.enter_context(
                nc.sbuf_tensor("trib_s", [128, 128], BF16))
            expS8 = stB2.enter_context(
                nc.sbuf_tensor("expS8", [128, 8 * S], BF16))
            expST2 = stB2.enter_context(
                nc.sbuf_tensor("expST2", [128, 2 * NT * W], BF16))
            diagP = stB2.enter_context(
                nc.sbuf_tensor("diagP", [128, 8 * 128], BF16))
            negm8 = stB2.enter_context(nc.sbuf_tensor("negm8", [128, 8], F32))
            den8 = stB2.enter_context(nc.sbuf_tensor("den8", [128, 8], F32))
            rcp8 = stB2.enter_context(nc.sbuf_tensor("rcp8", [128, 8], F32))
            mpart = stB2.enter_context(nc.sbuf_tensor("mpart", [128, 32], F32))
            dpart = stB2.enter_context(nc.sbuf_tensor("dpart", [128, 32], F32))

            E_trib = p.op("sync", lambda e: e.dma_start(
                out=trib_s[:], in_=tri_h), waits=[E_Q_all], inc="cst")

            ring_ap = ([psS[:, s * W:(s + 1) * W] for s in range(4)]
                       + [psU, psT0, psT1])
            ring_ev = [None, None, None, None, None,
                       bank_evac["psT0"], bank_evac["psT1"]]
            ring_i = [0]

            E_comb2 = [[None] * 4, [None] * 4]
            E_dsum2 = [[None] * 4, [None] * 4]
            E_expl2 = [[None] * 4, [None] * 4]
            E_exall2 = [[[] for _ in range(4)], [[] for _ in range(4)]]
            E_diag2v = [[None] * 4, [None] * 4]
            E_trmm2 = [[None] * 4, [None] * 4]
            E_trevV = [None, None]
            E_trevA = [None, None]
            E_trevall = [[], []]
            E_msd = [None, None]
            oev = [None]
            trctr = [0]

            def sc_mms(h, c, j):
                par = h % 2
                qt = 4 * c + j
                kext = (qt + 1) * 128
                nkc = (kext + W - 1) // W
                chunks = []
                for kc in range(nkc):
                    wdt = min(W, kext - kc * W)
                    s = ring_i[0] % 7
                    ring_i[0] += 1
                    slot = ring_ap[s]
                    w0 = [ring_ev[s], E_qevac[h][c], E_kevac[NCH - 1],
                          E_Q_all]
                    diag_chunk = (kc == nkc - 1)
                    p.op("tensor", lambda e, slot=slot, h=h, qt=qt, kc=kc,
                         wdt=wdt, dgc=diag_chunk: e.matmul(
                        slot[:, 0:wdt],
                        QTO[:, h * S + qt * 128: h * S + qt * 128 + 128],
                        KT[:, kc * W: kc * W + wdt],
                        start=True, stop=(not dgc), skip_group_check=True),
                        waits=w0,
                        inc=(None if diag_chunk else "t"))
                    if diag_chunk:
                        E_mm = p.op("tensor", lambda e, slot=slot, wdt=wdt:
                                    e.matmul(
                            slot[:, wdt - 128:wdt], ident_s[:], trib_s[:],
                            start=False, stop=True, skip_group_check=True),
                            waits=[E_trib], inc="t")
                    else:
                        E_mm = ("t", p.counts["t"])
                    chunks.append((s, slot, wdt, kc, E_mm))
                return chunks

            def sc_soft(h, c, j, chunks):
                par = h % 2
                qt = 4 * c + j
                kext = (qt + 1) * 128
                nkc = (kext + W - 1) // W
                mcol = par * 16 + j * 4
                ncol = par * 4 + j
                mx_all = []
                for s, slot, wdt, kc, E_mm in chunks:
                    mx_all.append(p.op("vector", lambda e, slot=slot, wdt=wdt,
                                mc=mcol + kc: e.reduce_max(
                        mpart[:, mc:mc + 1], slot[:, 0:wdt], axis=AX.X),
                        waits=[E_mm, E_comb2[par][j]], inc="v"))
                E_comb2[par][j] = p.op("vector", lambda e, mcol=mcol,
                                       nkc=nkc, ncol=ncol: e.reduce_max(
                    negm8[:, ncol:ncol + 1], mpart[:, mcol:mcol + nkc],
                    axis=AX.X, negate=True),
                    waits=mx_all + E_exall2[par][j], inc="v")
                E_ex = None
                ex_all = []
                for s, slot, wdt, kc, E_mm in chunks:
                    if nkc == 1:
                        acc = den8[:, ncol:ncol + 1]
                        wx = [E_comb2[par][j], E_trmm2[par][j],
                              E_diag2v[par][j]]
                    else:
                        acc = dpart[:, mcol + kc:mcol + kc + 1]
                        wx = [E_comb2[par][j], E_trmm2[par][j],
                              E_dsum2[par][j]]
                    E_ex = p.op("scalar", lambda e, slot=slot, wdt=wdt,
                                kc=kc, ncol=ncol, acc=acc, par=par, qt=qt:
                                e.activation(
                        expS8[:, (par * 4 + qt % 4) * S + kc * W:
                              (par * 4 + qt % 4) * S + kc * W + wdt],
                        slot[:, 0:wdt], AF.Exp,
                        bias=negm8[:, ncol:ncol + 1], scale=1.0,
                        accum_out=acc), waits=wx, inc="a")
                    ring_ev[s] = E_ex
                    ex_all.append(E_ex)
                E_expl2[par][j] = E_ex
                E_exall2[par][j] = list(ex_all)
                if nkc > 1:
                    E_dsum2[par][j] = p.op("vector", lambda e, mcol=mcol,
                                           nkc=nkc, ncol=ncol: e.reduce_sum(
                        den8[:, ncol:ncol + 1], dpart[:, mcol:mcol + nkc],
                        axis=AX.X),
                        waits=ex_all + [E_diag2v[par][j]], inc="v")
                E_den = E_dsum2[par][j] if nkc > 1 else E_expl2[par][j]
                E_rq = p.op("vector", lambda e, ncol=ncol: e.reciprocal(
                    rcp8[:, ncol:ncol + 1], den8[:, ncol:ncol + 1]),
                    waits=[E_den, E_diag2v[par][j]], inc="v")
                E_diag2v[par][j] = p.op("scalar", lambda e, par=par, j=j,
                                        ncol=ncol: e.activation(
                    diagP[:, (par * 4 + j) * 128:(par * 4 + j + 1) * 128],
                    ident_s[:], AF.Identity,
                    bias=0.0, scale=rcp8[:, ncol:ncol + 1]),
                    waits=[E_rq, E_trmm2[par][j]], inc="a")

            def tr_unit(h, c, j):
                par = h % 2
                qt = 4 * c + j
                base = par * NT * W
                if j == 0:
                    E_trevall[par] = []
                last_mm = None
                for g in range(c + 1):
                    cnt = min(4, qt + 1 - 4 * g)
                    s = ring_i[0] % 7
                    ring_i[0] += 1
                    bk = ring_ap[s]
                    w = [ring_ev[s], E_diag2v[par][j]] + E_exall2[par][j]
                    for i in range(cnt):
                        kt = 4 * g + i
                        last_mm = p.op("tensor",
                                       lambda e, bk=bk, i=i, kt=kt, par=par,
                                       qt=qt: e.matmul(
                            bk[:, i * 128:(i + 1) * 128],
                            expS8[:, (par * 4 + qt % 4) * S + kt * 128:
                                  (par * 4 + qt % 4) * S +
                                  (kt + 1) * 128],
                            diagP[:, (par * 4 + qt % 4) * 128:
                                  (par * 4 + qt % 4 + 1) * 128],
                            start=True, stop=True, skip_group_check=True),
                            waits=(w if i == 0 else ()),
                            inc=("t" if i == cnt - 1 else None))
                    dst = expST2[:, base:base + NT * W].rearrange(
                        "p (t w) -> p t w", t=NT)[
                        :, 4 * g:4 * g + cnt, j * 128:(j + 1) * 128]
                    srcv = bk[:, 0:cnt * 128].rearrange(
                        "p (t w) -> p t w", t=cnt)
                    eng = "vector" if trctr[0] % 2 == 0 else "scalar"
                    trctr[0] += 1
                    if eng == "vector":
                        ev = p.op("vector", lambda e, d=dst, s=srcv:
                                  e.tensor_copy(d, s), waits=[last_mm],
                                  inc="v")
                        E_trevV[par] = ev
                    else:
                        ev = p.op("scalar", lambda e, d=dst, s=srcv:
                                  e.copy(d, s), waits=[last_mm], inc="a")
                        E_trevA[par] = ev
                    E_trevall[par].append(ev)
                    ring_ev[s] = ev
                E_trmm2[par][j] = last_mm

            def av_unit(h, c, pj):
                par = h % 2
                nkt = 4 * c + 4
                base = par * NT * W
                w0 = []
                if pj == 0:
                    w0 = list(E_trevall[par]) + [E_msd[par], oev[0]]
                for i in range(4):
                    kt = 4 * pj + i
                    E_av = p.op("tensor", lambda e, kt=kt, nkt=nkt,
                                base=base: e.matmul(
                        psO[:],
                        Vsb[:, kt * 128:(kt + 1) * 128],
                        expST2[:, base + kt * W: base + (kt + 1) * W],
                        start=(kt == 0), stop=(kt == nkt - 1),
                        skip_group_check=True),
                        waits=(w0 if i == 0 else ()),
                        inc=("t" if kt == nkt - 1 else None))
                if pj == c:
                    oev[0] = p.op("scalar", lambda e, h=h, c=c: e.copy(
                        QTO[:, h * S + c * W: h * S + c * W + W], psO[:]),
                        waits=[E_av], inc="a")

            for c in range(NCH):
                for sl in range(2):
                    ev = None
                    for kt in range(4 * c + 1, 4 * c + 4):
                        for jj in range(kt - 4 * c):
                            ev = p.op("gpsimd", lambda e, sl=sl, kt=kt,
                                      jj=jj: e.memset(
                                expST2[:, sl * NT * W + kt * W + jj * 128:
                                       sl * NT * W + kt * W +
                                       (jj + 1) * 128], 0.0),
                                waits=[E_Q_all], inc="p")
                    E_msd[sl] = ev
                for step in range(H + 2):
                    for j in range(4):
                        if 1 <= step <= H:
                            tr_unit(step - 1, c, j)
                        if step >= 2 and j <= c:
                            av_unit(step - 2, c, j)
                        if step < H:
                            sc_soft(step, c, j, sc_mms(step, c, j))
            E_attn_done = oev[0]

            E_woe = [[None] * NT for _ in range(NCH)]
            E_fwe = [[None] * NT for _ in range(NCH)]
            E_xre = [[None] * NT for _ in range(NCH)]
            for ot in range(2):
                E_woe[0][ot] = p.op("sync", lambda e, ot=ot: e.dma_start(
                    out=wost[:, (ot % 2) * D:(ot % 2) * D + D],
                    in_=wo_bf[ot * 128:(ot + 1) * 128, :]),
                    waits=[E_Q_all, E_wob], inc="wo%d" % (ot % 2))
            xT4 = xT_h.rearrange("(t p) s -> t p s", p=128)
            for ot in range(3):
                E_xre[0][ot] = p.op("scalar", lambda e, ot=ot: e.dma_start(
                    out=xres[:, (ot % 3) * W:(ot % 3) * W + W],
                    in_=xT4[ot, :, 0:W]),
                    waits=[E_Q_all], inc="xr%d" % (ot % 3))

            stB2.close()
            stAB.close()
            stC = ExitStack()
            with stC:
                ycs = [stC.enter_context(
                           nc.sbuf_tensor("ycs0", [128, NT * W], F32)),
                       stC.enter_context(
                           nc.sbuf_tensor("ycs1", [128, NT * W], F32))]
                x1T = stC.enter_context(
                    nc.sbuf_tensor("x1T", [128, NT * W], BF16))
                castsc = stC.enter_context(
                    nc.sbuf_tensor("castsc", [128, 16 * W], BF16))
                fwst = stC.enter_context(
                    nc.sbuf_tensor("fwst", [128, 2 * D], BF16))
                yscr = stC.enter_context(
                    nc.sbuf_tensor("yscr", [128, 3 * W], F32))
                zscr = stC.enter_context(
                    nc.sbuf_tensor("zscr", [128, 3 * W], F32))

                pool_banks[:] = [("psT0", psT0), ("psT1", psT1),
                                 ("psU", psU), ("psO", psO)]
                bank_evac["psO"] = E_attn_done
                bank_evac["psT0"] = E_attn_done
                bank_evac["psT1"] = E_attn_done
                bank_evac["psU"] = E_attn_done

                OROT, OD = 5, 4
                FROT, FD = 3, 2
                OY, OSQ, FY, FSQ = 0, 5, 10, 13
                BCA, BCB = 1024, 1536
                XC, YC = 0, 512

                E_Omm = [[None] * NT for _ in range(NCH)]
                E_Fmm = [[None] * NT for _ in range(NCH)]
                E_evO = [[None] * NT for _ in range(NCH)]
                E_evF = [[None] * NT for _ in range(NCH)]
                E_addO = [[None] * NT for _ in range(NCH)]
                E_zadd = [[None] * NT for _ in range(NCH)]
                E_castO = [[None] * NT for _ in range(NCH)]
                E_sqO = [[None] * NT for _ in range(NCH)]
                E_castF = [[None] * NT for _ in range(NCH)]
                E_sqF = [[None] * NT for _ in range(NCH)]
                E_statO = [[None] * NT for _ in range(NCH)]
                E_statF = [[None] * NT for _ in range(NCH)]
                E_x1b = [[None] * NT for _ in range(NCH)]
                E_x1t = [[None] * NT for _ in range(NCH)]
                E_outa = [[None] * NT for _ in range(NCH)]
                E_lnrdO = [None] * NCH
                E_lnbO = [None] * NCH
                E_lnrdF = [None] * NCH
                E_lnbF = [None] * NCH
                E_abc1 = [None] * NCH
                E_abc2 = [None] * NCH
                last_bcb = [None]

                def prev_slot_user(ot, rot):
                    return ot + rot * ((NT - 1 - ot) // rot)

                def emit_stats(kind, c, j):
                    if kind == "O":
                        rp, ysl, qsl = 0, OY + j % OROT, OSQ + j % OROT
                        Ec, Es = E_castO[c][j], E_sqO[c][j]
                        w0 = []
                        if j == 0:
                            w0 = ([E_attn_done] if c == 0
                                  else [E_lnrdO[c - 1]])
                    else:
                        rp, ysl, qsl = 32, FY + j % FROT, FSQ + j % FROT
                        Ec, Es = E_castF[c][j], E_sqF[c][j]
                        w0 = []
                        if j == 0:
                            w0 = ([E_attn_done] if c == 0
                                  else [E_lnrdF[c - 1]])
                    p.op("tensor", lambda e, rp=rp, ysl=ysl: e.matmul(
                        psS[rp:rp + 1, 0:W], onesc_s[:],
                        castsc[:, ysl * W:(ysl + 1) * W],
                        start=(j == 0), stop=(j == NT - 1),
                        skip_group_check=True), waits=[Ec] + w0)
                    ev = p.op("tensor", lambda e, rp=rp, qsl=qsl: e.matmul(
                        psS[rp:rp + 1, W:2 * W], onesc_s[:],
                        castsc[:, qsl * W:(qsl + 1) * W],
                        start=(j == 0), stop=(j == NT - 1),
                        skip_group_check=True), waits=[Es] + w0, inc="t")
                    if kind == "O":
                        E_statO[c][j] = ev
                    else:
                        E_statF[c][j] = ev

                def emit_lnfin(kind, c):
                    rp = 0 if kind == "O" else 32
                    Estat = (E_statO if kind == "O" else E_statF)[c][NT - 1]
                    w0 = [Estat]
                    if last_bcb[0] is not None:
                        w0.append(last_bcb[0])
                    rX = rowb[rp:rp + 1, XC:XC + W]
                    rY = rowb[rp:rp + 1, YC:YC + W]
                    p1 = psS[rp:rp + 1, 0:W]
                    p2 = psS[rp:rp + 1, W:2 * W]
                    e1 = p.op("vector", lambda e: e.tensor_scalar(
                        rX, p1, 1.0 / 2048.0, None, AluOpType.mult),
                        waits=w0, inc="v")
                    e2 = p.op("vector", lambda e: e.tensor_scalar(
                        rY, p2, 1.0 / 2048.0, None, AluOpType.mult),
                        waits=[e1], inc="v")
                    e3 = p.op("vector", lambda e: e.tensor_tensor(
                        p1, rX, rX, AluOpType.mult), waits=[e2], inc="v")
                    e4 = p.op("vector", lambda e: e.tensor_tensor(
                        rY, rY, p1, AluOpType.subtract), waits=[e3], inc="v")
                    e5 = p.op("scalar", lambda e: e.activation(
                        p2, rY, AF.Ln, bias=eps1[rp:rp + 1, :],
                        scale=1.0), waits=[e4], inc="a")
                    e6 = p.op("scalar", lambda e: e.activation(
                        rY, p2, AF.Exp, bias=0.0, scale=-0.5),
                        waits=[e5], inc="a")
                    e7 = p.op("vector", lambda e: e.tensor_tensor(
                        rX, rX, rY, AluOpType.mult), waits=[e6], inc="v")
                    eb = p.op("vector", lambda e: e.tensor_scalar(
                        rX, rX, -1.0, None, AluOpType.mult),
                        waits=[e7], inc="v")
                    if kind == "O":
                        E_lnrdO[c], E_lnbO[c] = e6, eb
                    else:
                        E_lnrdF[c], E_lnbF[c] = e6, eb

                def emit_bcast_a(kind, c):
                    rp = 0 if kind == "O" else 32
                    Eb = (E_lnbO if kind == "O" else E_lnbF)[c]
                    w = [Eb]
                    if kind == "O":
                        w.append(E_abc2[c - 1] if c > 0 else E_attn_done)
                    else:
                        w.append(E_abc1[c])
                    p.op("tensor", lambda e, rp=rp: e.matmul(
                        psS[:, BCA:BCA + W], onesrF_s[rp:rp + 1, :],
                        rowb[rp:rp + 1, YC:YC + W], start=True, stop=True),
                        waits=w, inc="t")

                def emit_bcast_b(kind, c):
                    rp = 0 if kind == "O" else 32
                    ev = p.op("tensor", lambda e, rp=rp: e.matmul(
                        psS[:, BCB:BCB + W], onesrF_s[rp:rp + 1, :],
                        rowb[rp:rp + 1, XC:XC + W], start=True, stop=True),
                        inc="t")
                    last_bcb[0] = ev
                    return ev

                def emit_abc(kind, c, Ebcb):
                    if kind == "O":
                        at, bt = abc1_s, bbc1_s
                        war = E_x1b[c - 1][NT - 1] if c > 0 else None
                    else:
                        at, bt = abc2_s, bbc2_s
                        war = E_outa[c - 1][NT - 1] if c > 0 else None
                    p.op("vector", lambda e, at=at: e.tensor_copy(
                        at[:], psS[:, BCA:BCA + W]), waits=[Ebcb, war])
                    ev = p.op("vector", lambda e, bt=bt: e.tensor_copy(
                        bt[:], psS[:, BCB:BCB + W]), inc="v")
                    if kind == "O":
                        E_abc1[c] = ev
                    else:
                        E_abc2[c] = ev

                def emit_x1(c, ots):
                    pbuf = ycs[c % 2]
                    for ot in ots:
                        em = p.op("vector", lambda e, ot=ot, pbuf=pbuf:
                                  e.tensor_tensor(
                            pbuf[:, ot * W:(ot + 1) * W],
                            pbuf[:, ot * W:(ot + 1) * W], abc1_s[:],
                            AluOpType.mult),
                            waits=[E_abc1[c], E_castO[c][ot], E_sqO[c][ot]],
                            inc="v")
                        E_x1b[c][ot] = p.op("vector", lambda e, ot=ot,
                                            pbuf=pbuf: e.tensor_tensor(
                            pbuf[:, ot * W:(ot + 1) * W],
                            pbuf[:, ot * W:(ot + 1) * W], bbc1_s[:],
                            AluOpType.add), waits=[em], inc="v")

                def emit_x1t(c):
                    pbuf = ycs[c % 2]
                    wf = [E_Fmm[c - 1][NT - 1]] if c > 0 else []
                    for ot in range(NT):
                        E_x1t[c][ot] = p.op("scalar", lambda e, ot=ot,
                                            pbuf=pbuf: e.copy(
                            x1T[:, ot * W:(ot + 1) * W],
                            pbuf[:, ot * W:(ot + 1) * W]),
                            waits=[E_x1b[c][ot]] + wf, inc="a")

                def emit_out(c, ots):
                    pbuf = ycs[c % 2]
                    for ot in ots:
                        em = p.op("vector", lambda e, ot=ot, pbuf=pbuf:
                                  e.tensor_tensor(
                            pbuf[:, ot * W:(ot + 1) * W],
                            pbuf[:, ot * W:(ot + 1) * W], abc2_s[:],
                            AluOpType.mult),
                            waits=[E_abc2[c], E_castF[c][ot], E_sqF[c][ot]],
                            inc="v")
                        E_outa[c][ot] = p.op("vector", lambda e, ot=ot,
                                             pbuf=pbuf: e.tensor_tensor(
                            pbuf[:, ot * W:(ot + 1) * W],
                            pbuf[:, ot * W:(ot + 1) * W], bbc2_s[:],
                            AluOpType.add), waits=[em], inc="v")
                        wod = [E_outa[c][ot]]
                        if ot == 0:
                            wod.append(("od", 256 * c) if c > 0 else None)
                        p.op("sync", lambda e, c=c, ot=ot, pbuf=pbuf:
                             e.dma_start(
                            out=out_h[ot * 128:(ot + 1) * 128,
                                      c * W:(c + 1) * W],
                            in_=pbuf[:, ot * W:(ot + 1) * W]),
                            waits=wod, inc="od")

                def o_group(c, ot):
                    pbuf = ycs[c % 2]
                    if ot >= 2:
                        E_woe[c][ot] = p.op("sync", lambda e, c=c, ot=ot:
                                            e.dma_start(
                            out=wost[:, (ot % 2) * D:(ot % 2) * D + D],
                            in_=wo_bf[ot * 128:(ot + 1) * 128, :]),
                            waits=[E_Omm[c][ot - 2], E_wob],
                            inc="wo%d" % (ot % 2))
                    if ot >= 3:
                        E_xre[c][ot] = p.op("scalar", lambda e, c=c, ot=ot:
                                            e.dma_start(
                            out=xres[:, (ot % 3) * W:(ot % 3) * W + W],
                            in_=xT4[ot, :, c * W:(c + 1) * W]),
                            waits=[E_addO[c][ot - 3]], inc="xr%d" % (ot % 3))

                    def mmO(bk, w, c=c, ot=ot):
                        w = w + [E_woe[c][ot], E_attn_done]
                        for ht in range(NT):
                            ev = p.op("tensor", lambda e, bk=bk, ht=ht, c=c,
                                      ot=ot: e.matmul(
                                bk[:],
                                wost[:, (ot % 2) * D + ht * 128:
                                     (ot % 2) * D + ht * 128 + 128],
                                QTO[:, ht * S + c * W: ht * S + c * W + W],
                                start=(ht == 0), stop=(ht == NT - 1)),
                                waits=(w if ht == 0 else ()),
                                inc=("t" if ht == NT - 1 else None))
                        return ev

                    def evO(bk, w, c=c, ot=ot):
                        if ot >= 3:
                            w = w + [E_addO[c][ot - 3]]
                        elif c > 0:
                            w = w + [E_addO[c - 1][prev_slot_user(ot, 3)]]
                        return p.op("scalar", lambda e, bk=bk, c=c, ot=ot:
                                    e.activation(
                            yscr[:, (ot % 3) * W:(ot % 3) * W + W],
                            bk[:], AF.Identity, bias=ob_s[:, ot:ot + 1],
                            scale=1.0), waits=w, inc="a")
                    e_mm, e_ev = use_bank(mmO, evO)
                    E_Omm[c][ot] = e_mm
                    E_evO[c][ot] = e_ev
                    wadd = [e_ev, E_xre[c][ot]]
                    if c >= 2:
                        wadd.append(("od", 256 * (c - 1)))
                    E_addO[c][ot] = p.op("gpsimd", lambda e, c=c, ot=ot,
                                         pbuf=pbuf: e.tensor_tensor(
                        pbuf[:, ot * W:(ot + 1) * W],
                        yscr[:, (ot % 3) * W:(ot % 3) * W + W],
                        xres[:, (ot % 3) * W:(ot % 3) * W + W],
                        AluOpType.add), waits=wadd, inc="p")
                    wcs = [E_addO[c][ot]]
                    if ot >= OROT:
                        wcs.append(E_statO[c][ot - OROT])
                    elif c > 0:
                        wcs.append(E_statO[c - 1][prev_slot_user(ot, OROT)])
                    ysl = OY + ot % OROT
                    qsl = OSQ + ot % OROT
                    E_castO[c][ot] = p.op("gpsimd", lambda e, ot=ot, ysl=ysl,
                                          pbuf=pbuf: e.tensor_copy(
                        castsc[:, ysl * W:(ysl + 1) * W],
                        pbuf[:, ot * W:(ot + 1) * W]), waits=wcs, inc="p")
                    E_sqO[c][ot] = p.op("gpsimd", lambda e, ot=ot, qsl=qsl,
                                        pbuf=pbuf: e.tensor_tensor(
                        castsc[:, qsl * W:(qsl + 1) * W],
                        pbuf[:, ot * W:(ot + 1) * W],
                        pbuf[:, ot * W:(ot + 1) * W], AluOpType.mult),
                        waits=wcs, inc="p")
                    if ot >= OD:
                        emit_stats("O", c, ot - OD)

                def f_group(c, ot):
                    pbuf = ycs[c % 2]
                    if ot >= 2:
                        E_fwe[c][ot] = p.op("sync", lambda e, c=c, ot=ot:
                                            e.dma_start(
                            out=fwst[:, (ot % 2) * D:(ot % 2) * D + D],
                            in_=fw_bf[ot * 128:(ot + 1) * 128, :]),
                            waits=[E_Fmm[c][ot - 2], E_fwb],
                            inc="fw%d" % (ot % 2))

                    def mmF(bk, w, c=c, ot=ot):
                        w = w + [E_fwe[c][ot], E_x1t[c][NT - 1]]
                        for dt in range(NT):
                            ev = p.op("tensor", lambda e, bk=bk, dt=dt, c=c,
                                      ot=ot: e.matmul(
                                bk[:],
                                fwst[:, (ot % 2) * D + dt * 128:
                                     (ot % 2) * D + dt * 128 + 128],
                                x1T[:, dt * W:(dt + 1) * W],
                                start=(dt == 0), stop=(dt == NT - 1)),
                                waits=(w if dt == 0 else ()),
                                inc=("t" if dt == NT - 1 else None))
                        return ev

                    def evF(bk, w, c=c, ot=ot):
                        if ot >= 3:
                            w = w + [E_zadd[c][ot - 3]]
                        elif c > 0:
                            w = w + [E_zadd[c - 1][prev_slot_user(ot, 3)]]
                        return p.op("scalar", lambda e, bk=bk, c=c, ot=ot:
                                    e.activation(
                            zscr[:, (ot % 3) * W:(ot % 3) * W + W],
                            bk[:], AF.Identity, bias=fb_s[:, ot:ot + 1],
                            scale=fsc_s[:, ot:ot + 1]), waits=w, inc="a")
                    e_mm, e_ev = use_bank(mmF, evF)
                    E_Fmm[c][ot] = e_mm
                    E_evF[c][ot] = e_ev
                    E_zadd[c][ot] = p.op("gpsimd", lambda e, c=c, ot=ot,
                                         pbuf=pbuf: e.tensor_tensor(
                        pbuf[:, ot * W:(ot + 1) * W],
                        zscr[:, (ot % 3) * W:(ot % 3) * W + W],
                        pbuf[:, ot * W:(ot + 1) * W], AluOpType.add),
                        waits=[e_ev, E_x1t[c][ot]], inc="p")
                    wcs = [E_zadd[c][ot]]
                    if ot >= FROT:
                        wcs.append(E_statF[c][ot - FROT])
                    elif c > 0:
                        wcs.append(E_statF[c - 1][prev_slot_user(ot, FROT)])
                    ysl = FY + ot % FROT
                    qsl = FSQ + ot % FROT
                    E_castF[c][ot] = p.op("scalar", lambda e, ot=ot, ysl=ysl,
                                          pbuf=pbuf: e.copy(
                        castsc[:, ysl * W:(ysl + 1) * W],
                        pbuf[:, ot * W:(ot + 1) * W]), waits=wcs, inc="a")
                    E_sqF[c][ot] = p.op("gpsimd", lambda e, ot=ot, qsl=qsl,
                                        pbuf=pbuf: e.tensor_tensor(
                        castsc[:, qsl * W:(qsl + 1) * W],
                        pbuf[:, ot * W:(ot + 1) * W],
                        pbuf[:, ot * W:(ot + 1) * W], AluOpType.mult),
                        waits=wcs, inc="p")
                    if ot >= FD:
                        emit_stats("F", c, ot - FD)

                def make_tail(kind, c):
                    if kind == "O":
                        return [
                            lambda: (emit_stats("O", c, 12),
                                     emit_stats("O", c, 13),
                                     emit_stats("O", c, 14),
                                     emit_stats("O", c, 15)),
                            lambda: emit_lnfin("O", c),
                            lambda: emit_bcast_a("O", c),
                            lambda: emit_abc("O", c, emit_bcast_b("O", c)),
                            lambda: emit_x1(c, range(0, 4)),
                            lambda: emit_x1(c, range(4, 8)),
                            lambda: emit_x1(c, range(8, 12)),
                            lambda: emit_x1(c, range(12, 16)),
                        ], (lambda: emit_x1t(c))
                    else:
                        return [
                            lambda: (emit_stats("F", c, 14),
                                     emit_stats("F", c, 15)),
                            lambda: emit_lnfin("F", c),
                            lambda: emit_bcast_a("F", c),
                            lambda: emit_abc("F", c, emit_bcast_b("F", c)),
                            lambda: emit_out(c, range(0, 4)),
                            lambda: emit_out(c, range(4, 8)),
                            lambda: emit_out(c, range(8, 12)),
                            lambda: emit_out(c, range(12, 16)),
                        ], None

                def preload(kind, c):
                    if kind == "O":
                        for ot in range(2):
                            E_woe[c][ot] = p.op(
                                "sync", lambda e, c=c, ot=ot: e.dma_start(
                                    out=wost[:, (ot % 2) * D:
                                             (ot % 2) * D + D],
                                    in_=wo_bf[ot * 128:(ot + 1) * 128, :]),
                                waits=[E_Omm[c - 1][NT - 2 + ot], E_wob],
                                inc="wo%d" % (ot % 2))
                        for ot in range(3):
                            E_xre[c][ot] = p.op(
                                "scalar", lambda e, c=c, ot=ot: e.dma_start(
                                    out=xres[:, (ot % 3) * W:
                                             (ot % 3) * W + W],
                                    in_=xT4[ot, :, c * W:(c + 1) * W]),
                                waits=[E_addO[c - 1][prev_slot_user(ot, 3)]],
                                inc="xr%d" % (ot % 3))
                    else:
                        for ot in range(2):
                            wf = ([E_Fmm[c - 1][NT - 2 + ot]] if c > 0
                                  else [E_attn_done])
                            E_fwe[c][ot] = p.op(
                                "sync", lambda e, c=c, ot=ot: e.dma_start(
                                    out=fwst[:, (ot % 2) * D:
                                             (ot % 2) * D + D],
                                    in_=fw_bf[ot * 128:(ot + 1) * 128, :]),
                                waits=wf + [E_fwb], inc="fw%d" % (ot % 2))

                blocks = [("O", 0), ("O", 1), ("F", 0), ("O", 2),
                          ("F", 1), ("O", 3), ("F", 2), ("F", 3)]
                pend, pend_end = [], None
                for bi, (kind, c) in enumerate(blocks):
                    grp = o_group if kind == "O" else f_group
                    for ot in range(NT):
                        if ot < len(pend):
                            pend[ot]()
                        grp(c, ot)
                    for k in range(NT, len(pend)):
                        pend[k]()
                    if pend_end is not None:
                        pend_end()
                    if bi + 1 < len(blocks):
                        preload(*blocks[bi + 1])
                    pend, pend_end = make_tail(kind, c)
                for k in range(len(pend)):
                    pend[k]()

                p.op("gpsimd",
                     lambda e: e.wait_ge(sems["od"], 16 * NT * NCH))

        nc._prog = p
        with nc.Block() as block:
            block.sync(lambda e: p.emit("sync", e, sems))
            block.tensor(lambda e: p.emit("tensor", e, sems))
            block.vector(lambda e: p.emit("vector", e, sems))
            block.scalar(lambda e: p.emit("scalar", e, sems))
            block.gpsimd(lambda e: p.emit("gpsimd", e, sems))
    return nc


def stage_sq(w):
    w = np.asarray(w, np.float32)
    return np.ascontiguousarray(
        w.reshape(NT, 128, NT, 128).transpose(0, 3, 2, 1)).reshape(D, D)


def stage_kv(w):
    w = np.asarray(w, np.float32)
    return np.ascontiguousarray(
        w.reshape(128, NT, 128).transpose(2, 1, 0)).reshape(128, D)


def kernel(L0_attn_hidden_states, L0_attn_q_w, L0_attn_q_b, L0_attn_k_w,
           L0_attn_k_b, L0_attn_v_w, L0_attn_v_b, L0_attn_o_w, L0_attn_o_b,
           L0_ffn_weight, L0_ffn_scale_weight, L0_ffn_bias):
    x = np.asarray(L0_attn_hidden_states, np.float32)
    B = x.shape[0]

    qw = stage_sq(L0_attn_q_w)
    kw = stage_kv(L0_attn_k_w)
    vw = stage_kv(L0_attn_v_w)
    wo = stage_sq(L0_attn_o_w)
    fw = stage_sq(L0_ffn_weight)

    def btile(b):
        return np.ascontiguousarray(
            np.asarray(b, np.float32).reshape(NT, 128).T)

    qb = btile(L0_attn_q_b)
    ob = btile(L0_attn_o_b)
    fb = btile(L0_ffn_bias)
    fsc = btile(L0_ffn_scale_weight)
    kb = np.ascontiguousarray(np.asarray(L0_attn_k_b, np.float32)[:, None])
    vb = np.ascontiguousarray(np.asarray(L0_attn_v_b, np.float32)[:, None])

    qi = np.arange(128)
    tri = np.where(qi[None, :] <= qi[:, None], 0.0, NEG_BIG).astype(BF)
    ident = np.eye(128, dtype=BF)
    onesc = np.ones((128, 1), dtype=BF)
    onesrF = np.ones((33, 128), np.float32)

    nc = build_program()
    common = dict(qw=qw, kw=kw, vw=vw, wo=wo, fw=fw, qb=qb, ob=ob, fb=fb,
                  fsc=fsc, kb=kb, vb=vb, tri=tri, ident=ident, onesc=onesc,
                  onesrF=onesrF)
    in_maps = [dict(common, xT=np.ascontiguousarray(x[b].T)) for b in range(B)]
    res = run_bass_kernel_spmd(nc, in_maps, core_ids=list(range(B)))
    out = np.stack([np.asarray(r["outT"], np.float32).T
                    for r in res.results], axis=0)
    return np.ascontiguousarray(out)


# version 21
# speedup vs baseline: 1.1470x; 1.0058x over previous
import numpy as np
import ml_dtypes
from contextlib import ExitStack

import concourse.bass as bass
import concourse.mybir as mybir
from concourse.bass_utils import run_bass_kernel_spmd
from concourse.alu_op_type import AluOpType

F32 = mybir.dt.float32
BF16 = mybir.dt.bfloat16
AF = mybir.ActivationFunctionType
AX = mybir.AxisListType
BF = ml_dtypes.bfloat16

D = 2048
S = 2048
H = 16
DK = 128
NT = 16
NCH = 4
W = 512
INV_SQRT_DK = 1.0 / (DK ** 0.5)
NEG_BIG = -1.0e30
EPS = 1e-5

SEMS = {"t": 1, "v": 1, "a": 1, "p": 1,
        "cst": 16, "xbf": 16, "kvw": 16,
        "qw0": 16, "qw1": 16, "wo0": 16, "wo1": 16,
        "fw0": 16, "fw1": 16, "xr0": 16, "xr1": 16, "xr2": 16,
        "wob": 16, "fwb": 16, "od0": 16, "od1": 16, "od2": 16}


ENG_SEM = {"tensor": "t", "vector": "v", "scalar": "a", "gpsimd": "p",
           "sync": None}


class Prog:

    def __init__(self, serial=False):
        self.ops = []
        self.counts = {k: 0 for k in SEMS}
        self.serial = serial
        self.last = None

    def op(self, eng, fn, waits=(), inc=None):
        w = [e for e in waits if e is not None]
        if self.serial:
            if self.last is not None:
                w = w + [self.last]
            if inc is None and ENG_SEM[eng] is not None:
                inc = ENG_SEM[eng]
        ev = None
        if inc is not None:
            self.counts[inc] += SEMS[inc]
            ev = (inc, self.counts[inc])
        self.ops.append((eng, fn, w, inc))
        if self.serial and ev is not None:
            self.last = ev
        return ev

    def emit(self, eng_name, eng, sems):
        hi = {}
        for idx, (name, fn, waits, inc) in enumerate(self.ops):
            if name != eng_name:
                continue
            for s, v in waits:
                if hi.get(s, 0) < v:
                    eng.wait_ge(sems[s], v)
                    hi[s] = v
            ins = fn(eng)
            if inc is not None:
                if ins is None:
                    ins = eng.nop()
                ins.then_inc(sems[inc], SEMS[inc])


def build_program(serial=False):
    nc = bass.Bass("TRN2", target_bir_lowering=False)

    def dram(name, shape, dt, kind="ExternalInput"):
        return nc.dram_tensor(name, shape, dt, kind=kind).ap()

    xT_h = dram("xT", [D, S], F32)
    qw_h = dram("qw", [D, D], F32)
    kw_h = dram("kw", [128, D], F32)
    vw_h = dram("vw", [128, D], F32)
    wo_h = dram("wo", [D, D], F32)
    fw_h = dram("fw", [D, D], F32)
    qb_h = dram("qb", [128, NT], F32)
    ob_h = dram("ob", [128, NT], F32)
    fb_h = dram("fb", [128, NT], F32)
    fsc_h = dram("fsc", [128, NT], F32)
    kb_h = dram("kb", [128, 1], F32)
    vb_h = dram("vb", [128, 1], F32)
    tri_h = dram("tri", [128, 128], BF16)
    ident_h = dram("ident", [128, 128], BF16)
    onesc_h = dram("onesc", [128, 1], BF16)
    onesrF_h = dram("onesrF", [33, 128], F32)
    out_h = dram("outT", [D, S], F32, kind="ExternalOutput")
    wo_bf = dram("wo_bf", [D, D], BF16, kind="Internal")
    fw_bf = dram("fw_bf", [D, D], BF16, kind="Internal")

    p = Prog(serial=serial)
    st = ExitStack()
    sb = lambda n, s, d: st.enter_context(nc.sbuf_tensor(n, s, d))
    ps = lambda n, s, d: st.enter_context(nc.psum_tensor(n, s, d))

    with st:
        onesc_s = sb("onesc_s", [128, 1], BF16)
        onesrF_s = sb("onesrF_s", [33, 128], F32)
        qb_s = sb("qb_s", [128, NT], F32)
        qbs_s = sb("qbs_s", [128, NT], F32)
        ob_s = sb("ob_s", [128, NT], F32)
        fb_s = sb("fb_s", [128, NT], F32)
        fsc_s = sb("fsc_s", [128, NT], F32)
        kb_s = sb("kb_s", [128, 1], F32)
        vb_s = sb("vb_s", [128, 1], F32)
        rowb = sb("rowb", [33, 1024], F32)
        eps1 = sb("eps1", [33, 1], F32)
        abc1_s = sb("abc1_s", [128, W], F32)
        bbc1_s = sb("bbc1_s", [128, W], F32)
        abc2_s = sb("abc2_s", [128, W], F32)
        bbc2_s = sb("bbc2_s", [128, W], F32)
        wost = sb("wost", [128, 2 * D], BF16)
        xres = sb("xres", [128, 3 * W], F32)

        psS = ps("psS", [128, 2048], F32)
        psT0 = ps("psT0", [128, W], F32)
        psT1 = ps("psT1", [128, W], F32)
        psO = ps("psO", [128, W], F32)
        psU = ps("psU", [128, W], F32)
        psT = [psT0, psT1]

        sems = {k: st.enter_context(nc.semaphore(k)) for k in SEMS}

        pool_banks = [("psT0", psT0), ("psT1", psT1)]
        bank_evac = {"psT0": None, "psT1": None, "psO": None, "psU": None}
        rot_idx = [0]

        def use_bank(mm_emit, evac_emit):
            b = rot_idx[0] % len(pool_banks)
            rot_idx[0] += 1
            key, bk = pool_banks[b]
            mmw = [bank_evac[key]] if bank_evac[key] else []
            e_mm = mm_emit(bk, mmw)
            e_ev = evac_emit(bk, [e_mm])
            bank_evac[key] = e_ev
            return e_mm, e_ev

        with nc.sbuf_tensor("QTO", [128, NT * S], BF16) as QTO:
            stAB = ExitStack()
            ident_s = stAB.enter_context(
                nc.sbuf_tensor("ident_s", [128, 128], BF16))
            KT = stAB.enter_context(nc.sbuf_tensor("KT", [128, S], BF16))
            Vsb = stAB.enter_context(nc.sbuf_tensor("Vsb", [128, NT * 128], BF16))

            for dst, src in [(ident_s, ident_h),
                             (onesc_s, onesc_h), (onesrF_s, onesrF_h),
                             (qb_s, qb_h), (ob_s, ob_h), (fb_s, fb_h),
                             (fsc_s, fsc_h), (kb_s, kb_h), (vb_s, vb_h)]:
                E_consts = p.op("sync", lambda e, d=dst, s=src: e.dma_start(
                    out=d[:], in_=s), inc="cst")
            p.op("vector", lambda e: e.memset(eps1[:], EPS), inc="v")
            E_qbs = p.op("scalar", lambda e: e.mul(
                qbs_s[:], qb_s[:], INV_SQRT_DK), waits=[E_consts], inc="a")

            stA = ExitStack()
            with stA:
                xTbf = stA.enter_context(nc.sbuf_tensor("xTbf", [128, NT * S], BF16))
                qwst = stA.enter_context(nc.sbuf_tensor("qwst", [128, 2 * D], BF16))
                kwst = stA.enter_context(nc.sbuf_tensor("kwst", [128, NT * 128], BF16))
                vwst = stA.enter_context(nc.sbuf_tensor("vwst", [128, NT * 128], BF16))

                E_xbf = p.op("gpsimd", lambda e: e.dma_start(
                    out=xTbf[:].rearrange("p (t s) -> p t s", t=NT),
                    in_=xT_h.rearrange("(t p) s -> p t s", p=128)), inc="xbf")
                p.op("gpsimd", lambda e: e.dma_start(
                    out=kwst[:], in_=kw_h), inc="kvw")
                E_kvw = p.op("gpsimd", lambda e: e.dma_start(
                    out=vwst[:], in_=vw_h), inc="kvw")

                E_kevac = [None] * NCH
                for c in range(NCH):
                    def mmK(bk, w, c=c):
                        w = w + [E_xbf, E_kvw]
                        for dt in range(NT):
                            ev = p.op("tensor", lambda e, bk=bk, dt=dt, c=c: e.matmul(
                                bk[:],
                                kwst[:, dt * 128:(dt + 1) * 128],
                                xTbf[:, dt * S + c * W: dt * S + c * W + W],
                                start=(dt == 0), stop=(dt == NT - 1)),
                                waits=(w if dt == 0 else ()),
                                inc=("t" if dt == NT - 1 else None))
                        return ev

                    def evK(bk, w, c=c):
                        return p.op("scalar", lambda e, bk=bk, c=c: e.activation(
                            KT[:, c * W:(c + 1) * W], bk[:], AF.Identity,
                            bias=kb_s[:], scale=1.0),
                            waits=w + [E_consts], inc="a")
                    _, E_kevac[c] = use_bank(mmK, evK)

                E_vevac = [None] * NCH
                for c in range(NCH):
                    def mmV(bk, w, c=c):
                        w = w + [E_kvw]
                        for dt in range(NT):
                            ev = p.op("tensor", lambda e, bk=bk, dt=dt, c=c: e.matmul(
                                bk[:],
                                vwst[:, dt * 128:(dt + 1) * 128],
                                xTbf[:, dt * S + c * W: dt * S + c * W + W],
                                start=(dt == 0), stop=(dt == NT - 1)),
                                waits=(w if dt == 0 else ()),
                                inc=("t" if dt == NT - 1 else None))
                        return ev

                    def evV(bk, w, c=c):
                        return p.op("scalar", lambda e, bk=bk, c=c: e.activation(
                            QTO[:, 15 * S + c * W: 15 * S + (c + 1) * W],
                            bk[:], AF.Identity,
                            bias=vb_s[:], scale=1.0), waits=w, inc="a")
                    _, E_vevac[c] = use_bank(mmV, evV)

                E_vsb = None
                for g in range(4):
                    def mmVT(bk, w, g=g):
                        w = w + [E_vevac[g], E_consts]
                        for j in range(4):
                            kt = 4 * g + j
                            ev = p.op("tensor", lambda e, bk=bk, j=j, kt=kt: e.matmul(
                                bk[:, j * 128:(j + 1) * 128],
                                QTO[:, 15 * S + kt * 128:
                                    15 * S + (kt + 1) * 128],
                                ident_s[:], start=True, stop=True),
                                waits=(w if j == 0 else ()),
                                inc=("t" if j == 3 else None))
                        return ev

                    def evVT(bk, w, g=g):
                        return p.op("scalar", lambda e, bk=bk, g=g: e.copy(
                            Vsb[:, g * W:(g + 1) * W], bk[:]), waits=w, inc="a")
                    _, E_vsb = use_bank(mmVT, evVT)

                E_qw = [None] * NT
                E_qmm_last = [None] * NT
                E_qevac = [[None] * NCH for _ in range(NT)]
                for ot in range(NT):
                    wq = [E_qmm_last[ot - 2]] if ot >= 2 else []
                    E_qw[ot] = p.op("gpsimd", lambda e, ot=ot: e.dma_start(
                        out=qwst[:, (ot % 2) * D:(ot % 2) * D + D],
                        in_=qw_h[ot * 128:(ot + 1) * 128, :]),
                        waits=wq, inc="qw%d" % (ot % 2))
                    for c in range(NCH):
                        def mmQ(bk, w, ot=ot, c=c):
                            w = w + [E_qw[ot]]
                            for dt in range(NT):
                                ev = p.op("tensor", lambda e, bk=bk, dt=dt, ot=ot, c=c:
                                          e.matmul(
                                    bk[:],
                                    qwst[:, (ot % 2) * D + dt * 128:
                                         (ot % 2) * D + dt * 128 + 128],
                                    xTbf[:, dt * S + c * W: dt * S + c * W + W],
                                    start=(dt == 0), stop=(dt == NT - 1)),
                                    waits=(w if dt == 0 else ()),
                                    inc=("t" if dt == NT - 1 else None))
                            return ev

                        def evQ(bk, w, ot=ot, c=c):
                            if ot == 15:
                                w = w + [E_vsb]
                            return p.op("scalar", lambda e, bk=bk, ot=ot, c=c:
                                        e.activation(
                                QTO[:, ot * S + c * W: ot * S + c * W + W],
                                bk[:], AF.Identity,
                                bias=qbs_s[:, ot:ot + 1], scale=INV_SQRT_DK),
                                waits=w + [E_qbs], inc="a")
                        e_mm, e_ev = use_bank(mmQ, evQ)
                        E_qevac[ot][c] = e_ev
                        if c == NCH - 1:
                            E_qmm_last[ot] = e_mm
                E_Q_all = E_qmm_last[NT - 1]

            for q in range(4):
                p.op("gpsimd", lambda e, q=q: e.dma_start(
                    out=wo_bf[q * 512:(q + 1) * 512, :],
                    in_=wo_h[q * 512:(q + 1) * 512, :]), inc="wob")
                p.op("gpsimd", lambda e, q=q: e.dma_start(
                    out=fw_bf[q * 512:(q + 1) * 512, :],
                    in_=fw_h[q * 512:(q + 1) * 512, :]), inc="fwb")
            E_wob = ("wob", 64)
            E_fwb = ("fwb", 64)

            stB2 = ExitStack()
            trib_s = stB2.enter_context(
                nc.sbuf_tensor("trib_s", [128, 128], BF16))
            expS8 = stB2.enter_context(
                nc.sbuf_tensor("expS8", [128, 8 * S], BF16))
            expST2 = stB2.enter_context(
                nc.sbuf_tensor("expST2", [128, 2 * NT * W], BF16))
            diagP = stB2.enter_context(
                nc.sbuf_tensor("diagP", [128, 8 * 128], BF16))
            negm8 = stB2.enter_context(nc.sbuf_tensor("negm8", [128, 8], F32))
            den8 = stB2.enter_context(nc.sbuf_tensor("den8", [128, 8], F32))
            rcp8 = stB2.enter_context(nc.sbuf_tensor("rcp8", [128, 8], F32))
            mpart = stB2.enter_context(nc.sbuf_tensor("mpart", [128, 32], F32))
            dpart = stB2.enter_context(nc.sbuf_tensor("dpart", [128, 32], F32))

            E_trib = p.op("sync", lambda e: e.dma_start(
                out=trib_s[:], in_=tri_h), waits=[E_Q_all], inc="cst")

            ring_ap = ([psS[:, s * W:(s + 1) * W] for s in range(4)]
                       + [psU, psT0, psT1, psO])
            from collections import deque
            ring_free = deque([(0, None), (1, None), (2, None), (3, None),
                               (4, None), (5, bank_evac["psT0"]),
                               (6, bank_evac["psT1"]), (7, None)])
            av_slot = [None, None]

            E_comb2 = [[None] * 4, [None] * 4]
            E_dsum2 = [[None] * 4, [None] * 4]
            E_expl2 = [[None] * 4, [None] * 4]
            E_exall2 = [[[] for _ in range(4)], [[] for _ in range(4)]]
            E_diag2v = [[None] * 4, [None] * 4]
            E_trmm2 = [[None] * 4, [None] * 4]
            E_trevV = [None, None]
            E_trevA = [None, None]
            E_trevall = [[], []]
            E_msd = [None, None]
            oev = [None]
            trctr = [0]

            def sc_mms(h, c, j):
                par = h % 2
                qt = 4 * c + j
                kext = (qt + 1) * 128
                nkc = (kext + W - 1) // W
                chunks = []
                for kc in range(nkc):
                    wdt = min(W, kext - kc * W)
                    s, sev = ring_free.popleft()
                    slot = ring_ap[s]
                    w0 = [sev, E_qevac[h][c], E_kevac[NCH - 1],
                          E_Q_all]
                    diag_chunk = (kc == nkc - 1)
                    p.op("tensor", lambda e, slot=slot, h=h, qt=qt, kc=kc,
                         wdt=wdt, dgc=diag_chunk: e.matmul(
                        slot[:, 0:wdt],
                        QTO[:, h * S + qt * 128: h * S + qt * 128 + 128],
                        KT[:, kc * W: kc * W + wdt],
                        start=True, stop=(not dgc), skip_group_check=True),
                        waits=w0,
                        inc=(None if diag_chunk else "t"))
                    if diag_chunk:
                        E_mm = p.op("tensor", lambda e, slot=slot, wdt=wdt:
                                    e.matmul(
                            slot[:, wdt - 128:wdt], ident_s[:], trib_s[:],
                            start=False, stop=True, skip_group_check=True),
                            waits=[E_trib], inc="t")
                    else:
                        E_mm = ("t", p.counts["t"])
                    chunks.append((s, slot, wdt, kc, E_mm))
                return chunks

            def sc_soft(h, c, j, chunks):
                par = h % 2
                qt = 4 * c + j
                kext = (qt + 1) * 128
                nkc = (kext + W - 1) // W
                mcol = par * 16 + j * 4
                ncol = par * 4 + j
                mx_all = []
                for s, slot, wdt, kc, E_mm in chunks:
                    mx_all.append(p.op("vector", lambda e, slot=slot, wdt=wdt,
                                mc=mcol + kc: e.reduce_max(
                        mpart[:, mc:mc + 1], slot[:, 0:wdt], axis=AX.X),
                        waits=[E_mm, E_comb2[par][j]], inc="v"))
                E_comb2[par][j] = p.op("vector", lambda e, mcol=mcol,
                                       nkc=nkc, ncol=ncol: e.reduce_max(
                    negm8[:, ncol:ncol + 1], mpart[:, mcol:mcol + nkc],
                    axis=AX.X, negate=True),
                    waits=mx_all + E_exall2[par][j], inc="v")
                E_ex = None
                ex_all = []
                for s, slot, wdt, kc, E_mm in chunks:
                    if nkc == 1:
                        acc = den8[:, ncol:ncol + 1]
                        wx = [E_comb2[par][j], E_trmm2[par][j],
                              E_diag2v[par][j]]
                    else:
                        acc = dpart[:, mcol + kc:mcol + kc + 1]
                        wx = [E_comb2[par][j], E_trmm2[par][j],
                              E_dsum2[par][j]]
                    E_ex = p.op("scalar", lambda e, slot=slot, wdt=wdt,
                                kc=kc, ncol=ncol, acc=acc, par=par, qt=qt:
                                e.activation(
                        expS8[:, (par * 4 + qt % 4) * S + kc * W:
                              (par * 4 + qt % 4) * S + kc * W + wdt],
                        slot[:, 0:wdt], AF.Exp,
                        bias=negm8[:, ncol:ncol + 1], scale=1.0,
                        accum_out=acc), waits=wx, inc="a")
                    ring_free.append((s, E_ex))
                    ex_all.append(E_ex)
                E_expl2[par][j] = E_ex
                E_exall2[par][j] = list(ex_all)
                if nkc > 1:
                    E_dsum2[par][j] = p.op("vector", lambda e, mcol=mcol,
                                           nkc=nkc, ncol=ncol: e.reduce_sum(
                        den8[:, ncol:ncol + 1], dpart[:, mcol:mcol + nkc],
                        axis=AX.X),
                        waits=ex_all + [E_diag2v[par][j]], inc="v")
                E_den = E_dsum2[par][j] if nkc > 1 else E_expl2[par][j]
                E_rq = p.op("vector", lambda e, ncol=ncol: e.reciprocal(
                    rcp8[:, ncol:ncol + 1], den8[:, ncol:ncol + 1]),
                    waits=[E_den, E_diag2v[par][j]], inc="v")
                E_diag2v[par][j] = p.op("scalar", lambda e, par=par, j=j,
                                        ncol=ncol: e.activation(
                    diagP[:, (par * 4 + j) * 128:(par * 4 + j + 1) * 128],
                    ident_s[:], AF.Identity,
                    bias=0.0, scale=rcp8[:, ncol:ncol + 1]),
                    waits=[E_rq, E_trmm2[par][j]], inc="a")

            def tr_unit(h, c, j):
                par = h % 2
                qt = 4 * c + j
                base = par * NT * W
                if j == 0:
                    E_trevall[par] = []
                last_mm = None
                for g in range(c + 1):
                    cnt = min(4, qt + 1 - 4 * g)
                    s, sev = ring_free.popleft()
                    bk = ring_ap[s]
                    w = [sev, E_diag2v[par][j]] + E_exall2[par][j]
                    for i in range(cnt):
                        kt = 4 * g + i
                        last_mm = p.op("tensor",
                                       lambda e, bk=bk, i=i, kt=kt, par=par,
                                       qt=qt: e.matmul(
                            bk[:, i * 128:(i + 1) * 128],
                            expS8[:, (par * 4 + qt % 4) * S + kt * 128:
                                  (par * 4 + qt % 4) * S +
                                  (kt + 1) * 128],
                            diagP[:, (par * 4 + qt % 4) * 128:
                                  (par * 4 + qt % 4 + 1) * 128],
                            start=True, stop=True, skip_group_check=True),
                            waits=(w if i == 0 else ()),
                            inc=("t" if i == cnt - 1 else None))
                    dst = expST2[:, base:base + NT * W].rearrange(
                        "p (t w) -> p t w", t=NT)[
                        :, 4 * g:4 * g + cnt, j * 128:(j + 1) * 128]
                    srcv = bk[:, 0:cnt * 128].rearrange(
                        "p (t w) -> p t w", t=cnt)
                    eng = "vector" if trctr[0] % 2 == 0 else "scalar"
                    trctr[0] += 1
                    if eng == "vector":
                        ev = p.op("vector", lambda e, d=dst, s=srcv:
                                  e.tensor_copy(d, s), waits=[last_mm],
                                  inc="v")
                        E_trevV[par] = ev
                    else:
                        ev = p.op("scalar", lambda e, d=dst, s=srcv:
                                  e.copy(d, s), waits=[last_mm], inc="a")
                        E_trevA[par] = ev
                    E_trevall[par].append(ev)
                    ring_free.append((s, ev))
                E_trmm2[par][j] = last_mm

            def av_unit(h, c, pj):
                par = h % 2
                nkt = 4 * c + 4
                base = par * NT * W
                w0 = []
                if pj == 0:
                    s, sev = ring_free.popleft()
                    av_slot[par] = s
                    w0 = (list(E_trevall[par])
                          + [E_msd[par], sev])
                s = av_slot[par]
                bk = ring_ap[s]
                for i in range(4):
                    kt = 4 * pj + i
                    E_av = p.op("tensor", lambda e, bk=bk, kt=kt, nkt=nkt,
                                base=base: e.matmul(
                        bk[:, 0:W],
                        Vsb[:, kt * 128:(kt + 1) * 128],
                        expST2[:, base + kt * W: base + (kt + 1) * W],
                        start=(kt == 0), stop=(kt == nkt - 1),
                        skip_group_check=True),
                        waits=(w0 if i == 0 else ()),
                        inc=("t" if kt == nkt - 1 else None))
                if pj == c:
                    oev[0] = p.op("scalar", lambda e, bk=bk, h=h, c=c: e.copy(
                        QTO[:, h * S + c * W: h * S + c * W + W],
                        bk[:, 0:W]), waits=[E_av], inc="a")
                    ring_free.append((s, oev[0]))

            for c in range(NCH):
                for sl in range(2):
                    ev = None
                    for kt in range(4 * c + 1, 4 * c + 4):
                        for jj in range(kt - 4 * c):
                            ev = p.op("gpsimd", lambda e, sl=sl, kt=kt,
                                      jj=jj: e.memset(
                                expST2[:, sl * NT * W + kt * W + jj * 128:
                                       sl * NT * W + kt * W +
                                       (jj + 1) * 128], 0.0),
                                waits=[E_Q_all], inc="p")
                    E_msd[sl] = ev
                for step in range(H + 2):
                    for j in range(4):
                        if 1 <= step <= H:
                            tr_unit(step - 1, c, j)
                        if step >= 2 and j <= c:
                            av_unit(step - 2, c, j)
                        if step < H:
                            sc_soft(step, c, j, sc_mms(step, c, j))
            E_attn_done = oev[0]

            E_woe = [[None] * NT for _ in range(NCH)]
            E_fwe = [[None] * NT for _ in range(NCH)]
            E_xre = [[None] * NT for _ in range(NCH)]
            for ot in range(2):
                E_woe[0][ot] = p.op("sync", lambda e, ot=ot: e.dma_start(
                    out=wost[:, (ot % 2) * D:(ot % 2) * D + D],
                    in_=wo_bf[ot * 128:(ot + 1) * 128, :]),
                    waits=[E_Q_all, E_wob], inc="wo%d" % (ot % 2))
            xT4 = xT_h.rearrange("(t p) s -> t p s", p=128)
            for ot in range(3):
                E_xre[0][ot] = p.op("scalar", lambda e, ot=ot: e.dma_start(
                    out=xres[:, (ot % 3) * W:(ot % 3) * W + W],
                    in_=xT4[ot, :, 0:W]),
                    waits=[E_Q_all], inc="xr%d" % (ot % 3))

            stB2.close()
            stAB.close()
            stC = ExitStack()
            with stC:
                ycs = [stC.enter_context(
                           nc.sbuf_tensor("ycs0", [128, NT * W], F32)),
                       stC.enter_context(
                           nc.sbuf_tensor("ycs1", [128, NT * W], F32))]
                x1T = stC.enter_context(
                    nc.sbuf_tensor("x1T", [128, NT * W], BF16))
                castsc = stC.enter_context(
                    nc.sbuf_tensor("castsc", [128, 14 * W], BF16))
                fwst = stC.enter_context(
                    nc.sbuf_tensor("fwst", [128, 2 * D], BF16))
                yscr = stC.enter_context(
                    nc.sbuf_tensor("yscr", [128, 2 * W], F32))
                zscr = stC.enter_context(
                    nc.sbuf_tensor("zscr", [128, 2 * W], F32))
                outb = stC.enter_context(
                    nc.sbuf_tensor("outb", [128, 3 * W], F32))

                pool_banks[:] = [("psT0", psT0), ("psT1", psT1),
                                 ("psU", psU), ("psO", psO)]
                bank_evac["psO"] = E_attn_done
                bank_evac["psT0"] = E_attn_done
                bank_evac["psT1"] = E_attn_done
                bank_evac["psU"] = E_attn_done

                OROT, OD = 4, 3
                FROT, FD = 3, 2
                OY, OSQ, FY, FSQ = 0, 4, 8, 11
                BCA, BCB = 1024, 1536
                XC, YC = 0, 512

                E_Omm = [[None] * NT for _ in range(NCH)]
                E_Fmm = [[None] * NT for _ in range(NCH)]
                E_evO = [[None] * NT for _ in range(NCH)]
                E_evF = [[None] * NT for _ in range(NCH)]
                E_addO = [[None] * NT for _ in range(NCH)]
                E_zadd = [[None] * NT for _ in range(NCH)]
                E_castO = [[None] * NT for _ in range(NCH)]
                E_sqO = [[None] * NT for _ in range(NCH)]
                E_castF = [[None] * NT for _ in range(NCH)]
                E_sqF = [[None] * NT for _ in range(NCH)]
                E_statO = [[None] * NT for _ in range(NCH)]
                E_statF = [[None] * NT for _ in range(NCH)]
                E_x1b = [[None] * NT for _ in range(NCH)]
                E_outm = [[None] * NT for _ in range(NCH)]
                odc = [0, 0, 0]
                E_x1t = [[None] * NT for _ in range(NCH)]
                E_outa = [[None] * NT for _ in range(NCH)]
                E_lnrdO = [None] * NCH
                E_lnbO = [None] * NCH
                E_lnrdF = [None] * NCH
                E_lnbF = [None] * NCH
                E_abc1 = [None] * NCH
                E_abc2 = [None] * NCH
                last_bcb = [None]
                last_abc = [None]

                def prev_slot_user(ot, rot):
                    return ot + rot * ((NT - 1 - ot) // rot)

                def emit_stats(kind, c, j):
                    if kind == "O":
                        rp, ysl, qsl = 0, OY + j % OROT, OSQ + j % OROT
                        Ec, Es = E_castO[c][j], E_sqO[c][j]
                        w0 = []
                        if j == 0:
                            w0 = ([E_attn_done] if c == 0
                                  else [E_lnrdO[c - 1]])
                    else:
                        rp, ysl, qsl = 32, FY + j % FROT, FSQ + j % FROT
                        Ec, Es = E_castF[c][j], E_sqF[c][j]
                        w0 = []
                        if j == 0:
                            w0 = ([E_attn_done] if c == 0
                                  else [E_lnrdF[c - 1]])
                    p.op("tensor", lambda e, rp=rp, ysl=ysl: e.matmul(
                        psS[rp:rp + 1, 0:W], onesc_s[:],
                        castsc[:, ysl * W:(ysl + 1) * W],
                        start=(j == 0), stop=(j == NT - 1),
                        skip_group_check=True), waits=[Ec] + w0)
                    ev = p.op("tensor", lambda e, rp=rp, qsl=qsl: e.matmul(
                        psS[rp:rp + 1, W:2 * W], onesc_s[:],
                        castsc[:, qsl * W:(qsl + 1) * W],
                        start=(j == 0), stop=(j == NT - 1),
                        skip_group_check=True), waits=[Es] + w0, inc="t")
                    if kind == "O":
                        E_statO[c][j] = ev
                    else:
                        E_statF[c][j] = ev

                def emit_lnfin(kind, c):
                    rp = 0 if kind == "O" else 32
                    Estat = (E_statO if kind == "O" else E_statF)[c][NT - 1]
                    w0 = [Estat]
                    if last_bcb[0] is not None:
                        w0.append(last_bcb[0])
                    rX = rowb[rp:rp + 1, XC:XC + W]
                    rY = rowb[rp:rp + 1, YC:YC + W]
                    p1 = psS[rp:rp + 1, 0:W]
                    p2 = psS[rp:rp + 1, W:2 * W]
                    e1 = p.op("vector", lambda e: e.tensor_scalar(
                        rX, p1, 1.0 / 2048.0, None, AluOpType.mult),
                        waits=w0, inc="v")
                    e2 = p.op("vector", lambda e: e.tensor_scalar(
                        rY, p2, 1.0 / 2048.0, None, AluOpType.mult),
                        waits=[e1], inc="v")
                    e3 = p.op("vector", lambda e: e.tensor_tensor(
                        p1, rX, rX, AluOpType.mult), waits=[e2], inc="v")
                    e4 = p.op("vector", lambda e: e.tensor_tensor(
                        rY, rY, p1, AluOpType.subtract), waits=[e3], inc="v")
                    e5 = p.op("scalar", lambda e: e.activation(
                        p2, rY, AF.Ln, bias=eps1[rp:rp + 1, :],
                        scale=1.0), waits=[e4], inc="a")
                    e6 = p.op("scalar", lambda e: e.activation(
                        rY, p2, AF.Exp, bias=0.0, scale=-0.5),
                        waits=[e5], inc="a")
                    e7 = p.op("vector", lambda e: e.tensor_tensor(
                        rX, rX, rY, AluOpType.mult), waits=[e6], inc="v")
                    eb = p.op("vector", lambda e: e.tensor_scalar(
                        rX, rX, -1.0, None, AluOpType.mult),
                        waits=[e7], inc="v")
                    if kind == "O":
                        E_lnrdO[c], E_lnbO[c] = e6, eb
                    else:
                        E_lnrdF[c], E_lnbF[c] = e6, eb

                def emit_bcast_a(kind, c):
                    rp = 0 if kind == "O" else 32
                    Eb = (E_lnbO if kind == "O" else E_lnbF)[c]
                    w = [Eb, last_abc[0] if last_abc[0] else E_attn_done]
                    p.op("tensor", lambda e, rp=rp: e.matmul(
                        psS[:, BCA:BCA + W], onesrF_s[rp:rp + 1, :],
                        rowb[rp:rp + 1, YC:YC + W], start=True, stop=True),
                        waits=w, inc="t")

                def emit_bcast_b(kind, c):
                    rp = 0 if kind == "O" else 32
                    ev = p.op("tensor", lambda e, rp=rp: e.matmul(
                        psS[:, BCB:BCB + W], onesrF_s[rp:rp + 1, :],
                        rowb[rp:rp + 1, XC:XC + W], start=True, stop=True),
                        inc="t")
                    last_bcb[0] = ev
                    return ev

                def emit_abc(kind, c, Ebcb):
                    if kind == "O":
                        at, bt = abc1_s, bbc1_s
                        war = E_x1b[c - 1][NT - 1] if c > 0 else None
                    else:
                        at, bt = abc2_s, bbc2_s
                        war = E_outa[c - 1][NT - 1] if c > 0 else None
                    p.op("vector", lambda e, at=at: e.tensor_copy(
                        at[:], psS[:, BCA:BCA + W]), waits=[Ebcb, war])
                    ev = p.op("vector", lambda e, bt=bt: e.tensor_copy(
                        bt[:], psS[:, BCB:BCB + W]), inc="v")
                    last_abc[0] = ev
                    if kind == "O":
                        E_abc1[c] = ev
                    else:
                        E_abc2[c] = ev

                def emit_x1(c, ots):
                    pbuf = ycs[c % 2]
                    for ot in ots:
                        em = p.op("vector", lambda e, ot=ot, pbuf=pbuf:
                                  e.tensor_tensor(
                            pbuf[:, ot * W:(ot + 1) * W],
                            pbuf[:, ot * W:(ot + 1) * W], abc1_s[:],
                            AluOpType.mult),
                            waits=[E_abc1[c], E_castO[c][ot], E_sqO[c][ot]],
                            inc="v")
                        E_x1b[c][ot] = p.op("vector", lambda e, ot=ot,
                                            pbuf=pbuf: e.tensor_tensor(
                            pbuf[:, ot * W:(ot + 1) * W],
                            pbuf[:, ot * W:(ot + 1) * W], bbc1_s[:],
                            AluOpType.add), waits=[em], inc="v")

                def emit_x1t(c):
                    pbuf = ycs[c % 2]
                    wf = [E_Fmm[c - 1][NT - 1]] if c > 0 else []
                    for ot in range(NT):
                        E_x1t[c][ot] = p.op("scalar", lambda e, ot=ot,
                                            pbuf=pbuf: e.copy(
                            x1T[:, ot * W:(ot + 1) * W],
                            pbuf[:, ot * W:(ot + 1) * W]),
                            waits=[E_x1b[c][ot]] + wf, inc="a")

                def emit_out(c, ots):
                    pbuf = ycs[c % 2]
                    for ot in ots:
                        sl = ot % 3
                        wom = [E_abc2[c], E_castF[c][ot], E_sqF[c][ot]]
                        if odc[sl] > 0:
                            wom.append(("od%d" % sl, 16 * odc[sl]))
                        E_outm[c][ot] = p.op("vector", lambda e, ot=ot,
                                             pbuf=pbuf: e.tensor_tensor(
                            outb[:, (ot % 3) * W:(ot % 3 + 1) * W],
                            pbuf[:, ot * W:(ot + 1) * W], abc2_s[:],
                            AluOpType.mult), waits=wom, inc="v")
                        E_outa[c][ot] = p.op("vector", lambda e, ot=ot:
                                             e.tensor_tensor(
                            outb[:, (ot % 3) * W:(ot % 3 + 1) * W],
                            outb[:, (ot % 3) * W:(ot % 3 + 1) * W],
                            bbc2_s[:],
                            AluOpType.add), waits=[E_outm[c][ot]], inc="v")
                        p.op("sync", lambda e, c=c, ot=ot: e.dma_start(
                            out=out_h[ot * 128:(ot + 1) * 128,
                                      c * W:(c + 1) * W],
                            in_=outb[:, (ot % 3) * W:(ot % 3 + 1) * W]),
                            waits=[E_outa[c][ot]], inc="od%d" % sl)
                        odc[sl] += 1

                def o_group(c, ot):
                    pbuf = ycs[c % 2]
                    if ot >= 2:
                        E_woe[c][ot] = p.op("sync", lambda e, c=c, ot=ot:
                                            e.dma_start(
                            out=wost[:, (ot % 2) * D:(ot % 2) * D + D],
                            in_=wo_bf[ot * 128:(ot + 1) * 128, :]),
                            waits=[E_Omm[c][ot - 2], E_wob],
                            inc="wo%d" % (ot % 2))
                    if ot >= 3:
                        E_xre[c][ot] = p.op("scalar", lambda e, c=c, ot=ot:
                                            e.dma_start(
                            out=xres[:, (ot % 3) * W:(ot % 3) * W + W],
                            in_=xT4[ot, :, c * W:(c + 1) * W]),
                            waits=[E_addO[c][ot - 3]], inc="xr%d" % (ot % 3))

                    def mmO(bk, w, c=c, ot=ot):
                        w = w + [E_woe[c][ot], E_attn_done]
                        for ht in range(NT):
                            ev = p.op("tensor", lambda e, bk=bk, ht=ht, c=c,
                                      ot=ot: e.matmul(
                                bk[:],
                                wost[:, (ot % 2) * D + ht * 128:
                                     (ot % 2) * D + ht * 128 + 128],
                                QTO[:, ht * S + c * W: ht * S + c * W + W],
                                start=(ht == 0), stop=(ht == NT - 1)),
                                waits=(w if ht == 0 else ()),
                                inc=("t" if ht == NT - 1 else None))
                        return ev

                    def evO(bk, w, c=c, ot=ot):
                        if ot >= 2:
                            w = w + [E_addO[c][ot - 2]]
                        elif c > 0:
                            w = w + [E_addO[c - 1][prev_slot_user(ot, 2)]]
                        return p.op("scalar", lambda e, bk=bk, c=c, ot=ot:
                                    e.activation(
                            yscr[:, (ot % 2) * W:(ot % 2) * W + W],
                            bk[:], AF.Identity, bias=ob_s[:, ot:ot + 1],
                            scale=1.0), waits=w, inc="a")
                    e_mm, e_ev = use_bank(mmO, evO)
                    E_Omm[c][ot] = e_mm
                    E_evO[c][ot] = e_ev
                    wadd = [e_ev, E_xre[c][ot]]
                    if c >= 2:
                        wadd.append(E_outm[c - 2][ot])
                    E_addO[c][ot] = p.op("gpsimd", lambda e, c=c, ot=ot,
                                         pbuf=pbuf: e.tensor_tensor(
                        pbuf[:, ot * W:(ot + 1) * W],
                        yscr[:, (ot % 2) * W:(ot % 2) * W + W],
                        xres[:, (ot % 3) * W:(ot % 3) * W + W],
                        AluOpType.add), waits=wadd, inc="p")
                    wcs = [E_addO[c][ot]]
                    if ot >= OROT:
                        wcs.append(E_statO[c][ot - OROT])
                    elif c > 0:
                        wcs.append(E_statO[c - 1][prev_slot_user(ot, OROT)])
                    ysl = OY + ot % OROT
                    qsl = OSQ + ot % OROT
                    E_castO[c][ot] = p.op("gpsimd", lambda e, ot=ot, ysl=ysl,
                                          pbuf=pbuf: e.tensor_copy(
                        castsc[:, ysl * W:(ysl + 1) * W],
                        pbuf[:, ot * W:(ot + 1) * W]), waits=wcs, inc="p")
                    E_sqO[c][ot] = p.op("gpsimd", lambda e, ot=ot, qsl=qsl,
                                        pbuf=pbuf: e.tensor_tensor(
                        castsc[:, qsl * W:(qsl + 1) * W],
                        pbuf[:, ot * W:(ot + 1) * W],
                        pbuf[:, ot * W:(ot + 1) * W], AluOpType.mult),
                        waits=wcs, inc="p")
                    if ot >= OD:
                        emit_stats("O", c, ot - OD)

                def f_group(c, ot):
                    pbuf = ycs[c % 2]
                    if ot >= 2:
                        E_fwe[c][ot] = p.op("sync", lambda e, c=c, ot=ot:
                                            e.dma_start(
                            out=fwst[:, (ot % 2) * D:(ot % 2) * D + D],
                            in_=fw_bf[ot * 128:(ot + 1) * 128, :]),
                            waits=[E_Fmm[c][ot - 2], E_fwb],
                            inc="fw%d" % (ot % 2))

                    def mmF(bk, w, c=c, ot=ot):
                        w = w + [E_fwe[c][ot], E_x1t[c][NT - 1]]
                        for dt in range(NT):
                            ev = p.op("tensor", lambda e, bk=bk, dt=dt, c=c,
                                      ot=ot: e.matmul(
                                bk[:],
                                fwst[:, (ot % 2) * D + dt * 128:
                                     (ot % 2) * D + dt * 128 + 128],
                                x1T[:, dt * W:(dt + 1) * W],
                                start=(dt == 0), stop=(dt == NT - 1)),
                                waits=(w if dt == 0 else ()),
                                inc=("t" if dt == NT - 1 else None))
                        return ev

                    def evF(bk, w, c=c, ot=ot):
                        if ot >= 2:
                            w = w + [E_zadd[c][ot - 2]]
                        elif c > 0:
                            w = w + [E_zadd[c - 1][prev_slot_user(ot, 2)]]
                        return p.op("scalar", lambda e, bk=bk, c=c, ot=ot:
                                    e.activation(
                            zscr[:, (ot % 2) * W:(ot % 2) * W + W],
                            bk[:], AF.Identity, bias=fb_s[:, ot:ot + 1],
                            scale=fsc_s[:, ot:ot + 1]), waits=w, inc="a")
                    e_mm, e_ev = use_bank(mmF, evF)
                    E_Fmm[c][ot] = e_mm
                    E_evF[c][ot] = e_ev
                    E_zadd[c][ot] = p.op("gpsimd", lambda e, c=c, ot=ot,
                                         pbuf=pbuf: e.tensor_tensor(
                        pbuf[:, ot * W:(ot + 1) * W],
                        zscr[:, (ot % 2) * W:(ot % 2) * W + W],
                        pbuf[:, ot * W:(ot + 1) * W], AluOpType.add),
                        waits=[e_ev, E_x1t[c][ot]], inc="p")
                    wcs = [E_zadd[c][ot]]
                    if ot >= FROT:
                        wcs.append(E_statF[c][ot - FROT])
                    elif c > 0:
                        wcs.append(E_statF[c - 1][prev_slot_user(ot, FROT)])
                    ysl = FY + ot % FROT
                    qsl = FSQ + ot % FROT
                    E_castF[c][ot] = p.op("scalar", lambda e, ot=ot, ysl=ysl,
                                          pbuf=pbuf: e.copy(
                        castsc[:, ysl * W:(ysl + 1) * W],
                        pbuf[:, ot * W:(ot + 1) * W]), waits=wcs, inc="a")
                    E_sqF[c][ot] = p.op("gpsimd", lambda e, ot=ot, qsl=qsl,
                                        pbuf=pbuf: e.tensor_tensor(
                        castsc[:, qsl * W:(qsl + 1) * W],
                        pbuf[:, ot * W:(ot + 1) * W],
                        pbuf[:, ot * W:(ot + 1) * W], AluOpType.mult),
                        waits=wcs, inc="p")
                    if ot >= FD:
                        emit_stats("F", c, ot - FD)

                def make_tail(kind, c):
                    if kind == "O":
                        return [
                            lambda: (emit_stats("O", c, 13),
                                     emit_stats("O", c, 14),
                                     emit_stats("O", c, 15)),
                            lambda: emit_lnfin("O", c),
                            lambda: emit_bcast_a("O", c),
                            lambda: emit_abc("O", c, emit_bcast_b("O", c)),
                            lambda: emit_x1(c, range(0, 4)),
                            lambda: emit_x1(c, range(4, 8)),
                            lambda: emit_x1(c, range(8, 12)),
                            lambda: emit_x1(c, range(12, 16)),
                        ], (lambda: emit_x1t(c))
                    else:
                        return [
                            lambda: (emit_stats("F", c, 14),
                                     emit_stats("F", c, 15)),
                            lambda: emit_lnfin("F", c),
                            lambda: emit_bcast_a("F", c),
                            lambda: emit_abc("F", c, emit_bcast_b("F", c)),
                            lambda: emit_out(c, range(0, 4)),
                            lambda: emit_out(c, range(4, 8)),
                            lambda: emit_out(c, range(8, 12)),
                            lambda: emit_out(c, range(12, 16)),
                        ], None

                def preload(kind, c):
                    if kind == "O":
                        for ot in range(2):
                            E_woe[c][ot] = p.op(
                                "sync", lambda e, c=c, ot=ot: e.dma_start(
                                    out=wost[:, (ot % 2) * D:
                                             (ot % 2) * D + D],
                                    in_=wo_bf[ot * 128:(ot + 1) * 128, :]),
                                waits=[E_Omm[c - 1][NT - 2 + ot], E_wob],
                                inc="wo%d" % (ot % 2))
                        for ot in range(3):
                            E_xre[c][ot] = p.op(
                                "scalar", lambda e, c=c, ot=ot: e.dma_start(
                                    out=xres[:, (ot % 3) * W:
                                             (ot % 3) * W + W],
                                    in_=xT4[ot, :, c * W:(c + 1) * W]),
                                waits=[E_addO[c - 1][prev_slot_user(ot, 3)]],
                                inc="xr%d" % (ot % 3))
                    else:
                        for ot in range(2):
                            wf = ([E_Fmm[c - 1][NT - 2 + ot]] if c > 0
                                  else [E_attn_done])
                            E_fwe[c][ot] = p.op(
                                "sync", lambda e, c=c, ot=ot: e.dma_start(
                                    out=fwst[:, (ot % 2) * D:
                                             (ot % 2) * D + D],
                                    in_=fw_bf[ot * 128:(ot + 1) * 128, :]),
                                waits=wf + [E_fwb], inc="fw%d" % (ot % 2))

                blocks = [("O", 0), ("O", 1), ("F", 0), ("F", 1),
                          ("O", 2), ("O", 3), ("F", 2), ("F", 3)]
                pend, pend_end = [], None
                for bi, (kind, c) in enumerate(blocks):
                    grp = o_group if kind == "O" else f_group
                    for ot in range(NT):
                        if ot < len(pend):
                            pend[ot]()
                        grp(c, ot)
                    for k in range(NT, len(pend)):
                        pend[k]()
                    if pend_end is not None:
                        pend_end()
                    if bi + 1 < len(blocks):
                        preload(*blocks[bi + 1])
                    pend, pend_end = make_tail(kind, c)
                for k in range(len(pend)):
                    pend[k]()

                p.op("gpsimd", lambda e: (
                    e.wait_ge(sems["od0"], 16 * odc[0]),
                    e.wait_ge(sems["od1"], 16 * odc[1]),
                    e.wait_ge(sems["od2"], 16 * odc[2]))[-1])

        nc._prog = p
        with nc.Block() as block:
            block.sync(lambda e: p.emit("sync", e, sems))
            block.tensor(lambda e: p.emit("tensor", e, sems))
            block.vector(lambda e: p.emit("vector", e, sems))
            block.scalar(lambda e: p.emit("scalar", e, sems))
            block.gpsimd(lambda e: p.emit("gpsimd", e, sems))
    return nc


def stage_sq(w):
    w = np.asarray(w, np.float32)
    return np.ascontiguousarray(
        w.reshape(NT, 128, NT, 128).transpose(0, 3, 2, 1)).reshape(D, D)


def stage_kv(w):
    w = np.asarray(w, np.float32)
    return np.ascontiguousarray(
        w.reshape(128, NT, 128).transpose(2, 1, 0)).reshape(128, D)


def kernel(L0_attn_hidden_states, L0_attn_q_w, L0_attn_q_b, L0_attn_k_w,
           L0_attn_k_b, L0_attn_v_w, L0_attn_v_b, L0_attn_o_w, L0_attn_o_b,
           L0_ffn_weight, L0_ffn_scale_weight, L0_ffn_bias):
    x = np.asarray(L0_attn_hidden_states, np.float32)
    B = x.shape[0]

    qw = stage_sq(L0_attn_q_w)
    kw = stage_kv(L0_attn_k_w)
    vw = stage_kv(L0_attn_v_w)
    wo = stage_sq(L0_attn_o_w)
    fw = stage_sq(L0_ffn_weight)

    def btile(b):
        return np.ascontiguousarray(
            np.asarray(b, np.float32).reshape(NT, 128).T)

    qb = btile(L0_attn_q_b)
    ob = btile(L0_attn_o_b)
    fb = btile(L0_ffn_bias)
    fsc = btile(L0_ffn_scale_weight)
    kb = np.ascontiguousarray(np.asarray(L0_attn_k_b, np.float32)[:, None])
    vb = np.ascontiguousarray(np.asarray(L0_attn_v_b, np.float32)[:, None])

    qi = np.arange(128)
    tri = np.where(qi[None, :] <= qi[:, None], 0.0, NEG_BIG).astype(BF)
    ident = np.eye(128, dtype=BF)
    onesc = np.ones((128, 1), dtype=BF)
    onesrF = np.ones((33, 128), np.float32)

    nc = build_program()
    common = dict(qw=qw, kw=kw, vw=vw, wo=wo, fw=fw, qb=qb, ob=ob, fb=fb,
                  fsc=fsc, kb=kb, vb=vb, tri=tri, ident=ident, onesc=onesc,
                  onesrF=onesrF)
    in_maps = [dict(common, xT=np.ascontiguousarray(x[b].T)) for b in range(B)]
    res = run_bass_kernel_spmd(nc, in_maps, core_ids=list(range(B)))
    out = np.stack([np.asarray(r["outT"], np.float32).T
                    for r in res.results], axis=0)
    return np.ascontiguousarray(out)
